# version 01
# baseline (speedup 1.0000x reference)
import numpy as np
import ml_dtypes
from contextlib import ExitStack
import concourse.bass as bass
import concourse.mybir as mybir
from concourse.bass_utils import run_bass_kernel_spmd

F32 = mybir.dt.float32
BF16 = mybir.dt.bfloat16
AF = mybir.ActivationFunctionType
ALU = mybir.AluOpType
AX = mybir.AxisListType

D = 2048
T = 2048
NCHUNK = 16
NT = 4
TW = 512
MAGIC = float(np.float32(2.0 ** 23 + 2.0 ** 22))
LN_EPS = 1e-5
RMS_EPS = float(np.finfo(np.float32).eps)


class Buf:
    __slots__ = ("w", "r")

    def __init__(self):
        self.w = None
        self.r = []


class Rec:

    def __init__(self, nc, es):
        self.nc = nc
        self.es = es
        self.ops = {e: [] for e in "tvsgy"}
        self.sems = {}
        self.cnt = {}
        self.lastwait = {}
        self.bufs = {}
        self.dma_slots = set()

    def B(self, *key):
        if key not in self.bufs:
            self.bufs[key] = Buf()
        return self.bufs[key]

    def sem(self, name):
        if name not in self.sems:
            self.sems[name] = self.es.enter_context(self.nc.semaphore(name))
            self.cnt[name] = 0
        return self.sems[name]

    def op(self, e, emit, reads=(), writes=(), deps=(), sem=None, inc=1, waw=True):
        semname = sem if sem is not None else ("E" + e)
        self.sem(semname)
        if inc == 16:
            self.dma_slots.add(semname)
        dl = []
        for b in reads:
            dl.append(b.w)
        for b in writes:
            if waw:
                dl.append(b.w)
            dl.extend(b.r)
        dl.extend(deps)
        waits = []
        for d in dl:
            if d is None:
                continue
            sn, val = d
            if e == "t" and sn == "Et":
                continue
            key = (e, sn)
            if self.lastwait.get(key, 0) >= val:
                continue
            self.lastwait[key] = val
            waits.append((sn, val))
        self.cnt[semname] += inc
        tok = (semname, self.cnt[semname])
        self.ops[e].append((waits, emit, semname, inc))
        for b in reads:
            b.r.append(tok)
        for b in writes:
            b.w = tok
            b.r = []
        return tok

    def barrier(self):
        targets = [(sn, self.cnt[sn]) for sn in list(self.cnt)
                   if self.cnt[sn] > 0]
        for e in "tvsgy":
            def emit(eng):
                return eng.nop()
            self.op(e, emit, deps=targets)

    def replay(self, e, eng):
        for waits, emit, semname, inc in self.ops[e]:
            for sn, val in waits:
                eng.wait_ge(self.sems[sn], val)
            ins = emit(eng)
            if ins is not None:
                ins.then_inc(self.sems[semname], inc)


class K:
    def __init__(self, nc, es):
        self.nc = nc
        self.es = es
        self.r = Rec(nc, es)

    def alloc(self, name, shape, dtype, stack=None):
        return (stack or self.es).enter_context(self.nc.sbuf_tensor(name, shape, dtype))

    def do(self, e, fn, **kw):
        return self.r.op(e, fn, **kw)

    def dma(self, e, dst_ap, src_ap, slot, reads=(), writes=(), deps=(), transpose=False, waw=True, slow=False):
        def emit(eng):
            if slow:
                return eng.dma_start(dst_ap, src_ap, transpose=transpose,
                                     allow_slow_non_contiguous=True)
            return eng.dma_start(dst_ap, src_ap, transpose=transpose)
        return self.r.op(e, emit, reads=reads, writes=writes, deps=deps, sem=slot,
                         inc=16, waw=waw)

    def dma_group(self, e, items, slot, reads=(), writebufs=(), deps=()):
        tok = None
        first = True
        for dst_ap, src_ap, tr in items:
            def emit(eng, d=dst_ap, s=src_ap, t=tr):
                return eng.dma_start(d, s, transpose=t)
            tok = self.r.op(e, emit, reads=reads if first else (),
                            writes=writebufs if first else (), deps=deps if first else (),
                            sem=slot, inc=16, waw=first)
            first = False
        for b in writebufs:
            b.w = tok
            b.r = []
        for b in reads:
            b.r.append(tok)
        return tok

    def mm(self, out_ap, lhsT_ap, rhs_ap, start, stop, bankbuf, reads=(), deps=(), tile_position=None):
        def emit(eng):
            return eng.matmul(out_ap, lhsT_ap, rhs_ap, start=start, stop=stop,
                              tile_position=tile_position)
        return self.r.op("t", emit, reads=reads, writes=(bankbuf,), deps=deps)

    def act(self, out_ap, in_ap, func, reads=(), writes=(), deps=(), bias=0.0, scale=1.0):
        def emit(eng):
            return eng.activation(out_ap, in_ap, func, bias=bias, scale=scale)
        return self.r.op("s", emit, reads=reads, writes=writes, deps=deps)

    def vop(self, name, *args, reads=(), writes=(), deps=(), engine="v", **kw):
        def emit(eng):
            return getattr(eng, name)(*args, **kw)
        return self.r.op(engine, emit, reads=reads, writes=writes, deps=deps)


def build(phases=("l0_attn", "l0_moe", "l1_attn", "l1_ffn"), nc_cls=None):
    if nc_cls is None:
        nc_cls = bass.Bass
    nc = nc_cls("TRN2", target_bir_lowering=False)

    x = nc.dram_tensor("x", [T, D], F32, kind="ExternalInput").ap()
    w_qkv = nc.dram_tensor("w_qkv", [D, 3 * D], F32, kind="ExternalInput").ap()
    b_qkv = nc.dram_tensor("b_qkv", [3 * D], F32, kind="ExternalInput").ap()
    w_out = nc.dram_tensor("w_out", [D, D], F32, kind="ExternalInput").ap()
    b_out = nc.dram_tensor("b_out", [D], F32, kind="ExternalInput").ap()
    router_w = nc.dram_tensor("router_w", [8, D], F32, kind="ExternalInput").ap()
    w1 = nc.dram_tensor("w1", [8, 2816, D], F32, kind="ExternalInput").ap()
    w2 = nc.dram_tensor("w2", [8, D, 1408], F32, kind="ExternalInput").ap()
    Wq = nc.dram_tensor("Wq", [D, D], F32, kind="ExternalInput").ap()
    Wk = nc.dram_tensor("Wk", [D, 512], F32, kind="ExternalInput").ap()
    Wv = nc.dram_tensor("Wv", [D, 512], F32, kind="ExternalInput").ap()
    Wo = nc.dram_tensor("Wo", [D, D], F32, kind="ExternalInput").ap()
    wgu = nc.dram_tensor("wgu", [16384, D], F32, kind="ExternalInput").ap()
    wp = nc.dram_tensor("wp", [D, 8192], F32, kind="ExternalInput").ap()
    out = nc.dram_tensor("out", [T, D], F32, kind="ExternalOutput").ap()
    sc_wgu = nc.dram_tensor("sc_wgu", [128, D, 128], BF16, kind="Internal").ap()
    sc_wp = nc.dram_tensor("sc_wp", [16, 8192, 128], BF16, kind="Internal").ap()

    es = ExitStack()
    k = K(nc, es)
    r = k.r
    B = r.B

    R = k.alloc("R", [128, NCHUNK, T], BF16)
    ones_col = k.alloc("ones_col", [128, 1], BF16)
    ones_row = k.alloc("ones_row", [1, 128], F32)
    rows = k.alloc("rows", [1, 5, TW], F32)
    smallf = k.alloc("smallf", [128, 32], F32)
    zr = k.alloc("zr", [1, TW], F32)
    zb = k.alloc("zb", [128, TW], BF16)
    negc = k.alloc("negc", [128, 1], F32)
    PS = es.enter_context(nc.psum_tensor("PS", [128, 4096], F32))

    def bank(i, w=TW):
        return PS[:, i * TW:i * TW + w]

    def bankrow(i, p0, p1, w=TW):
        return PS[p0:p1, i * TW:i * TW + w]

    k.vop("memset", ones_col[:], 1.0, writes=(B("ones_col"),))
    k.vop("memset", ones_row[:], 1.0, writes=(B("ones_row"),))
    CONSTS = [LN_EPS, RMS_EPS, 1.0 / float(np.sqrt(128.0)), 0.125, 1.0, 0.0]
    ctile = k.alloc("ctile", [128, len(CONSTS)], F32)
    for ci, cv in enumerate(CONSTS):
        k.vop("memset", ctile[:, ci:ci + 1], float(cv), writes=(B("ctile"),))
        nc.const_aps.register(F32, float(cv), ctile[:, ci:ci + 1])
    r.barrier()

    def rsqrt_rows(tag, src_psum, src_buf, scale, eps):
        S2 = rows[:, 1, :]
        MU = rows[:, 2, :]
        TMP = rows[:, 3, :]
        M1 = rows[:, 4, :]
        k.vop("tensor_scalar_mul", S2, src_psum, scale,
              reads=(src_buf,), writes=(B(tag, "s2"),))
        k.act(MU, S2, AF.Sqrt, bias=eps, reads=(B(tag, "s2"),), writes=(B(tag, "mu"),))
        k.vop("reciprocal", TMP, MU, reads=(B(tag, "mu"),), writes=(B(tag, "tmp"),))
        k.vop("tensor_mul", M1, S2, TMP, reads=(B(tag, "s2"), B(tag, "tmp")),
              writes=(B(tag, "m1"),))
        k.vop("scalar_tensor_tensor", M1, TMP, eps, M1, ALU.mult, ALU.add,
              reads=(B(tag, "tmp"), B(tag, "m1")), writes=(B(tag, "m1"),))
        k.vop("tensor_mul", M1, M1, TMP, reads=(B(tag, "m1"), B(tag, "tmp")),
              writes=(B(tag, "m1"),))
        k.vop("tensor_scalar", M1, M1, -0.5, 1.5, ALU.mult, ALU.add,
              reads=(B(tag, "m1"),), writes=(B(tag, "m1"),))
        k.vop("tensor_mul", M1, M1, TMP, reads=(B(tag, "m1"), B(tag, "tmp")),
              writes=(B(tag, "m1"),))

    def layernorm(xn, xnname, sq, sqname, tag, eps=LN_EPS, tts=None, toff=0):
        S1 = rows[:, 0, :]
        S2 = rows[:, 1, :]
        MU = rows[:, 2, :]
        TMP = rows[:, 3, :]
        M1 = rows[:, 4, :]
        M0 = rows[:, 1, :]
        if tts is None:
            tts = range(NT)
        for tt in tts:
            ts = slice(tt * TW, (tt + 1) * TW)
            xts = slice((tt - toff) * TW, (tt - toff + 1) * TW)
            for j in range(NCHUNK):
                k.mm(bankrow(0, 0, 1), ones_col[:], R[:, j, ts], j == 0, j == NCHUNK - 1,
                     B("psum", 0), reads=(B("ones_col"), B("R", j, tt)))
            for j in range(NCHUNK):
                sl = j % 2
                k.vop("tensor_mul", sq[:, sl, :], R[:, j, ts], R[:, j, ts],
                      reads=(B("R", j, tt),), writes=(B(sqname, sl),))
                k.mm(bankrow(1, 0, 1), ones_col[:], sq[:, sl, :], j == 0, j == NCHUNK - 1,
                     B("psum", 1), reads=(B("ones_col"), B(sqname, sl)))
            k.vop("tensor_scalar_mul", S1, bankrow(0, 0, 1), 1.0 / D,
                  reads=(B("psum", 0),), writes=(B(tag, "s1"),))
            k.vop("tensor_scalar_mul", S2, bankrow(1, 0, 1), 1.0 / D,
                  reads=(B("psum", 1),), writes=(B(tag, "s2"),))
            k.vop("tensor_mul", MU, S1, S1, reads=(B(tag, "s1"),), writes=(B(tag, "mu"),))
            k.vop("tensor_sub", S2, S2, MU, reads=(B(tag, "s2"), B(tag, "mu")),
                  writes=(B(tag, "s2"),))
            k.act(MU, S2, AF.Sqrt, bias=eps, reads=(B(tag, "s2"),), writes=(B(tag, "mu"),))
            k.vop("reciprocal", TMP, MU, reads=(B(tag, "mu"),), writes=(B(tag, "tmp"),))
            k.vop("tensor_mul", M1, S2, TMP, reads=(B(tag, "s2"), B(tag, "tmp")),
                  writes=(B(tag, "m1"),))
            k.vop("scalar_tensor_tensor", M1, TMP, eps, M1, ALU.mult, ALU.add,
                  reads=(B(tag, "tmp"), B(tag, "m1")), writes=(B(tag, "m1"),))
            k.vop("tensor_mul", M1, M1, TMP, reads=(B(tag, "m1"), B(tag, "tmp")),
                  writes=(B(tag, "m1"),))
            k.vop("tensor_scalar", M1, M1, -0.5, 1.5, ALU.mult, ALU.add,
                  reads=(B(tag, "m1"),), writes=(B(tag, "m1"),))
            k.vop("tensor_mul", M1, M1, TMP, reads=(B(tag, "m1"), B(tag, "tmp")),
                  writes=(B(tag, "m1"),))
            k.vop("tensor_mul", M0, S1, M1, reads=(B(tag, "s1"), B(tag, "m1")),
                  writes=(B(tag, "s2"),))
            k.vop("tensor_scalar_mul", M0, M0, -1.0, reads=(B(tag, "s2"),),
                  writes=(B(tag, "s2"),))
            k.mm(bank(2), ones_row[:], M1, True, True, B("psum", 2),
                 reads=(B("ones_row"), B(tag, "m1")))
            k.mm(bank(3), ones_row[:], M0, True, True, B("psum", 3),
                 reads=(B("ones_row"), B(tag, "s2")))
            for j in range(NCHUNK):
                sl = j % 2
                k.vop("tensor_mul", sq[:, sl, :], R[:, j, ts], bank(2),
                      reads=(B("R", j, tt), B("psum", 2)), writes=(B(sqname, sl),))
                k.vop("tensor_add", xn[:, j, xts], sq[:, sl, :], bank(3),
                      reads=(B(sqname, sl), B("psum", 3)), writes=(B(xnname, j, tt),))

    with ExitStack() as ph:
        stg = k.alloc("stg_in", [128, 1, D], BF16, stack=ph)
        for tt in range(16):
            sl = 0
            stgap = stg[:, sl, :]
            k.dma("g", stgap, x[tt * 128:(tt + 1) * 128, :], f"dstg{sl}",
                  writes=(B("stg_in", sl),))
            k.dma_group("y",
                        [(R[:, j, tt * 128:(tt + 1) * 128], stgap[:, j * 128:(j + 1) * 128], True)
                         for j in range(NCHUNK)],
                        f"dRfill{sl}", reads=(B("stg_in", sl),),
                        writebufs=tuple(B("R", j, tt // 4) for j in range(NCHUNK)))

        if "l0_attn" in phases:
            xn = k.alloc("xn0", [128, NCHUNK, T], BF16, stack=ph)
            sq = k.alloc("sq0", [128, 2, TW], BF16, stack=ph)
            Qh = k.alloc("Qh", [128, T], BF16, stack=ph)
            Kh = k.alloc("Kh", [128, T], BF16, stack=ph)
            Vtm = k.alloc("Vtm", [128, NCHUNK, 128], BF16, stack=ph)
            Pp = k.alloc("Pp", [128, NCHUNK, TW], BF16, stack=ph)
            Oq = k.alloc("Oq", [128, 4, T], BF16, stack=ph)
            WS = k.alloc("WS", [128, 3, NCHUNK, 128], BF16, stack=ph)
            WO = k.alloc("WO", [128, 2, 4, 128], BF16, stack=ph)
            biasq = k.alloc("biasq", [128, 4], F32, stack=ph)
            bo = k.alloc("bo", [128, NCHUNK], F32, stack=ph)

            layernorm(xn, "xn0", sq, "sq0", "ln0")
            k.dma("g", bo[:], bass.AP(b_out.tensor, 0, [[1, 128], [128, NCHUNK]]),
                  "dbo", writes=(B("bo"),), slow=True)

            SC = 1.0 / float(np.sqrt(128.0))

            def load_qkv_slice(h, wi):
                sl = (h * 3 + wi) % 3
                c0 = wi * D + h * 128
                src = bass.AP(w_qkv.tensor, c0, [[3 * D, 128], [128 * 3 * D, NCHUNK], [1, 128]])
                k.dma("g", WS[:, sl, :, :], src, f"dws{sl}", writes=(B("WS", sl),))
                return sl

            for h in range(16):
                slq = load_qkv_slice(h, 0)
                slk = load_qkv_slice(h, 1)
                slv = load_qkv_slice(h, 2)
                k.dma("g", biasq[:, 0:3],
                      bass.AP(b_qkv.tensor, h * 128, [[1, 128], [D, 3]]),
                      "dbq", writes=(B("biasq"),), slow=True)

                for (dst, dstn, wsl, wi, bki) in ((Qh, "Qh", slq, 0, 0), (Kh, "Kh", slk, 1, 1)):
                    for tt in range(NT):
                        for j in range(NCHUNK):
                            k.mm(bank(bki), WS[:, wsl, j, :], xn[:, j, tt * TW:(tt + 1) * TW],
                                 j == 0, j == NCHUNK - 1, B("psum", bki),
                                 reads=(B("WS", wsl), B("xn0", j, tt)))
                        k.act(dst[:, tt * TW:(tt + 1) * TW], bank(bki), AF.Identity,
                              bias=biasq[:, wi:wi + 1],
                              reads=(B("psum", bki), B("biasq")), writes=(B(dstn, tt),))
                for kc in range(NCHUNK):
                    bk = 2 + kc % 2
                    for j in range(NCHUNK):
                        k.mm(bank(bk, 128), xn[:, j, kc * 128:(kc + 1) * 128], WS[:, slv, j, :],
                             j == 0, j == NCHUNK - 1, B("psum", bk),
                             reads=(B("WS", slv), B("xn0", j, kc // 4)))
                    k.act(Vtm[:, kc, :], bank(bk, 128), AF.Identity, bias=biasq[:, 2:3],
                          reads=(B("psum", bk), B("biasq")), writes=(B("Vtm", kc),))

                mx = []
                for (src, srcn) in ((Qh, "Qh"), (Kh, "Kh")):
                    for tt in range(NT):
                        ts = slice(tt * TW, (tt + 1) * TW)
                        sl = tt % 2
                        k.vop("tensor_mul", sq[:, sl, :], src[:, ts], src[:, ts],
                              reads=(B(srcn, tt),), writes=(B("sq0", sl),))
                        k.mm(bankrow(4, 0, 1), ones_col[:], sq[:, sl, :],
                             True, True, B("psum", 4), reads=(B("ones_col"), B("sq0", sl)))
                        ix = len(mx)
                        k.vop("tensor_reduce", smallf[0:1, ix:ix + 1], bankrow(4, 0, 1),
                              AX.X, ALU.max, reads=(B("psum", 4),), writes=(B("smx", ix),))
                        mx.append(B("smx", ix))
                k.vop("tensor_reduce", smallf[0:1, 8:9], smallf[0:1, 0:8], AX.X, ALU.max,
                      reads=tuple(mx), writes=(B("cmax"),))
                k.vop("tensor_scalar", smallf[0:1, 9:10], smallf[0:1, 8:9], -SC, -4.0,
                      ALU.mult, ALU.add, reads=(B("cmax"),), writes=(B("cneg"),))
                k.mm(PS[:, 4 * TW:4 * TW + 1], ones_row[:], smallf[0:1, 9:10], True, True,
                     B("psum", 4), reads=(B("ones_row"), B("cneg")))
                k.act(negc[:], PS[:, 4 * TW:4 * TW + 1], AF.Copy,
                      reads=(B("psum", 4),), writes=(B("negc"),))

                hq = h % 4
                for tt in range(NT):
                    ts = slice(tt * TW, (tt + 1) * TW)
                    for j in range(NCHUNK):
                        bk = j % 4
                        k.mm(bank(bk), Kh[:, j * 128:(j + 1) * 128], Qh[:, ts], True, True,
                             B("psum", bk), reads=(B("Kh", j // 4), B("Qh", tt)))
                        if j % 2 == 1:
                            b0 = bk - 1
                            k.act(Pp[:, j - 1:j + 1, :].rearrange("p a b -> p (a b)"),
                                  PS[:, b0 * TW:(b0 + 2) * TW], AF.Exp,
                                  bias=negc[:], scale=SC,
                                  reads=(B("psum", b0), B("psum", bk), B("negc")),
                                  writes=(B("Pp", j - 1), B("Pp", j)))
                    for j in range(NCHUNK):
                        k.mm(bankrow(5, 0, 1), ones_col[:], Pp[:, j, :], j == 0, j == NCHUNK - 1,
                             B("psum", 5), reads=(B("ones_col"), B("Pp", j)))
                        k.mm(bank(6), Vtm[:, j, :], Pp[:, j, :], j == 0, j == NCHUNK - 1,
                             B("psum", 6), reads=(B("Vtm", j), B("Pp", j)))
                    k.vop("reciprocal", zr[:], bankrow(5, 0, 1), reads=(B("psum", 5),),
                          writes=(B("zr"),))
                    k.mm(bank(7), ones_row[:], zr[:], True, True, B("psum", 7),
                         reads=(B("ones_row"), B("zr")))
                    k.act(zb[:], bank(7), AF.Copy, reads=(B("psum", 7),), writes=(B("zb"),))
                    k.vop("tensor_mul", Oq[:, hq, ts], bank(6), zb[:],
                          reads=(B("psum", 6), B("zb")), writes=(B("Oq", hq, tt),))

                if hq == 3:
                    qq = h // 4
                    for jd in range(NCHUNK):
                        wsl = jd % 2
                        src = bass.AP(w_out.tensor, (qq * 4 * 128) * D + jd * 128,
                                      [[D, 128], [128 * D, 4], [1, 128]])
                        k.dma("g", WO[:, wsl, :, :], src, f"dwo{wsl}", writes=(B("WO", wsl),))
                        for tt in range(NT):
                            bk = 5 + tt % 2
                            ts = slice(tt * TW, (tt + 1) * TW)
                            for fc in range(4):
                                k.mm(bank(bk), WO[:, wsl, fc, :], Oq[:, fc, ts],
                                     fc == 0, fc == 3, B("psum", bk),
                                     reads=(B("WO", wsl), B("Oq", fc, tt)))
                            if qq == 0:
                                k.vop("scalar_tensor_tensor", R[:, jd, ts], bank(bk),
                                      bo[:, jd:jd + 1], R[:, jd, ts], ALU.add, ALU.add,
                                      reads=(B("psum", bk), B("bo"), B("R", jd, tt)),
                                      writes=(B("R", jd, tt),))
                            else:
                                k.vop("tensor_add", R[:, jd, ts], bank(bk), R[:, jd, ts],
                                      reads=(B("psum", bk), B("R", jd, tt)),
                                      writes=(B("R", jd, tt),))
        r.barrier()

    if "l0_moe" in phases:
        with ExitStack() as ph:
            xn = k.alloc("xn1", [128, NCHUNK, T], BF16, stack=ph)
            sq = k.alloc("sq1", [128, 2, TW], BF16, stack=ph)
            glog = k.alloc("glog", [128, NCHUNK, 8], F32, stack=ph)
            gx = k.alloc("gx", [128, 2, 8], F32, stack=ph)
            gfm = k.alloc("gfm", [128, T], BF16, stack=ph)
            ones_rowb = k.alloc("ones_rowb", [1, 128], BF16, stack=ph)
            sel = k.alloc("sel", [8, 8, 128], BF16, stack=ph)
            ph2 = ExitStack()
            rwS = k.alloc("rwS", [16, D], BF16, stack=ph2)
            rwT = k.alloc("rwT", [128, NCHUNK, 16], BF16, stack=ph2)
            gsc = k.alloc("gsc", [128, 8], F32, stack=ph2)
            gcast = k.alloc("gcast", [128, 128], BF16, stack=ph2)

            k.vop("memset", ones_rowb[:], 1.0, writes=(B("ones_rowb"),))
            k.vop("memset", gcast[:, 8:128], 0.0, writes=(B("gcz"),))
            sel_np = np.zeros((8, 8, 128), dtype=ml_dtypes.bfloat16)
            for e8 in range(8):
                sel_np[e8, e8, :] = 1.0
            sel_const = nc.inline_tensor(sel_np.reshape(8, 8 * 128), name="sel_const")
            k.dma("g", sel[:].rearrange("p a b -> p (a b)"), sel_const.ap(), "dsel",
                  writes=(B("sel"),))
            layernorm(xn, "xn1", sq, "sq1", "ln1")

            k.vop("memset", rwS[:], 0.0, writes=(B("rwSz"),))
            k.dma("g", rwS[0:8, :], router_w, "drw", deps=(r.B("rwSz").w,),
                  writes=(B("rwS"),))
            k.dma_group("y",
                        [(rwT[:, j, :], rwS[:, j * 128:(j + 1) * 128], True)
                         for j in range(NCHUNK)],
                        "drwT", reads=(B("rwS"), B("rwSz")),
                        writebufs=(B("rwT"),))

            for tc in range(NCHUNK):
                cs = slice(tc * 128, (tc + 1) * 128)
                for j in range(NCHUNK):
                    k.mm(PS[:, 4 * TW:4 * TW + 8], xn[:, j, cs], rwT[:, j, 0:8],
                         j == 0, j == NCHUNK - 1, B("psum", 4),
                         reads=(B("xn1", j, tc // 4), B("rwT")))
                k.act(glog[:, tc, :], PS[:, 4 * TW:4 * TW + 8], AF.Copy,
                      reads=(B("psum", 4),), writes=(B("glog", tc),))
                L = glog[:, tc, :]
                EQ = gx[:, 0, :]
                L2 = gx[:, 1, :]
                k.vop("tensor_reduce", gsc[:, 0:1], L, AX.X, ALU.max,
                      reads=(B("glog", tc),), writes=(B("gm1"),))
                k.vop("tensor_single_scalar", EQ, L, gsc[:, 0:1], ALU.is_ge,
                      reads=(B("glog", tc), B("gm1")), writes=(B("geq"),))
                k.vop("scalar_tensor_tensor", L2, EQ, -1e9, L, ALU.mult, ALU.add,
                      reads=(B("geq"), B("glog", tc)), writes=(B("gl2"),))
                k.vop("tensor_reduce", gsc[:, 1:2], L2, AX.X, ALU.max,
                      reads=(B("gl2"),), writes=(B("gm2"),))
                k.vop("tensor_single_scalar", EQ, L, gsc[:, 1:2], ALU.is_ge,
                      reads=(B("glog", tc), B("gm2")), writes=(B("geq"),))
                k.vop("tensor_single_scalar", gsc[:, 2:3], gsc[:, 0:1], -1.0, ALU.mult,
                      reads=(B("gm1"),), writes=(B("gnm1"),))
                k.act(L2, L, AF.Exp, bias=gsc[:, 2:3],
                      reads=(B("glog", tc), B("gnm1")), writes=(B("gl2"),))
                k.vop("tensor_mul", L, L2, EQ,
                      reads=(B("gl2"), B("geq")), writes=(B("glog", tc),))
                k.vop("tensor_reduce", gsc[:, 3:4], L, AX.X, ALU.add,
                      reads=(B("glog", tc),), writes=(B("gz"),))
                k.vop("reciprocal", gsc[:, 3:4], gsc[:, 3:4],
                      reads=(B("gz"),), writes=(B("gz"),))
                k.vop("tensor_single_scalar", gcast[:, 0:8], L, gsc[:, 3:4], ALU.mult,
                      reads=(B("glog", tc), B("gz")), writes=(B("gc8"),))
                k.dma_group("y", [(gfm[:, cs], gcast[:], True)], "dgfm",
                            reads=(B("gc8"), B("gcz")),
                            writebufs=(B("gfm", tc),))

            ph2.close()
            yact = k.alloc("yact", [128, 11, T // 2], BF16, stack=ph)
            W1S = k.alloc("W1S", [128, 2, D], BF16, stack=ph)
            W1T = k.alloc("W1T", [128, 4, NCHUNK, 128], BF16, stack=ph)
            W2S = k.alloc("W2S", [128, 1, 1408], BF16, stack=ph)
            W2T = k.alloc("W2T", [128, 2, 11, 128], BF16, stack=ph)
            gb = k.alloc("gb", [128, T // 2], BF16, stack=ph)

            for ex in range(8):
                for hf in range(2):
                    hs = slice(hf * (T // 2), (hf + 1) * (T // 2))
                    for q in range(2):
                        tt = hf * 2 + q
                        ts = slice(tt * TW, (tt + 1) * TW)
                        k.mm(bank(6), sel[0:8, ex, :], gfm[0:8, ts], True, True,
                             B("psum", 6), reads=(B("sel"), B("gfm", tt * 4),
                                                  B("gfm", tt * 4 + 1), B("gfm", tt * 4 + 2),
                                                  B("gfm", tt * 4 + 3)))
                        k.act(gb[:, q * TW:(q + 1) * TW], bank(6), AF.Copy,
                              reads=(B("psum", 6),), writes=(B("gb", q),))
                    for c in range(11):
                        slA = (c % 2) * 2
                        slB = slA + 1
                        for (rowbase, sl) in ((c * 128, slA), (1408 + c * 128, slB)):
                            ss = 0 if sl < 2 else 1
                            k.dma("g", W1S[:, ss, :],
                                  bass.AP(w1.tensor, ex * 2816 * D + rowbase * D,
                                          [[D, 128], [1, D]]),
                                  f"dW1S{ss}", writes=(B("W1S", ss),))
                            k.dma_group("y",
                                        [(W1T[:, sl, j, :], W1S[:, ss, j * 128:(j + 1) * 128], True)
                                         for j in range(NCHUNK)],
                                        f"dW1T{ss}", reads=(B("W1S", ss),),
                                        writebufs=(B("W1T", sl),))
                        for q in range(2):
                            tt = hf * 2 + q
                            ts = slice(tt * TW, (tt + 1) * TW)
                            lq = slice(q * TW, (q + 1) * TW)
                            bkG = q * 2
                            bkU = q * 2 + 1
                            for j in range(NCHUNK):
                                k.mm(bank(bkG), W1T[:, slA, j, :], xn[:, j, ts],
                                     j == 0, j == NCHUNK - 1, B("psum", bkG),
                                     reads=(B("W1T", slA), B("xn1", j, tt)))
                            for j in range(NCHUNK):
                                k.mm(bank(bkU), W1T[:, slB, j, :], xn[:, j, ts],
                                     j == 0, j == NCHUNK - 1, B("psum", bkU),
                                     reads=(B("W1T", slB), B("xn1", j, tt)))
                            k.act(sq[:, 0, :], bank(bkG), AF.Silu,
                                  reads=(B("psum", bkG),), writes=(B("sq1", 0),))
                            k.vop("tensor_mul", sq[:, 1, :], sq[:, 0, :], bank(bkU),
                                  reads=(B("sq1", 0), B("psum", bkU)), writes=(B("sq1", 1),))
                            k.vop("tensor_mul", yact[:, c, lq], sq[:, 1, :],
                                  gb[:, q * TW:(q + 1) * TW],
                                  reads=(B("sq1", 1), B("gb", q)), writes=(B("yact", c),))
                    for hc in range(NCHUNK):
                        sl = hc % 2
                        k.dma("g", W2S[:, 0, :],
                              bass.AP(w2.tensor, ex * D * 1408 + hc * 128 * 1408,
                                      [[1408, 128], [1, 1408]]),
                              "dW2S0", writes=(B("W2S", 0),))
                        k.dma_group("y",
                                    [(W2T[:, sl, i, :], W2S[:, 0, i * 128:(i + 1) * 128], True)
                                     for i in range(11)],
                                    f"dW2T{sl}", reads=(B("W2S", 0),),
                                    writebufs=(B("W2T", sl),))
                        for q in range(2):
                            tt = hf * 2 + q
                            ts = slice(tt * TW, (tt + 1) * TW)
                            lq = slice(q * TW, (q + 1) * TW)
                            bk = 4 + q
                            for i in range(11):
                                k.mm(bank(bk), W2T[:, sl, i, :], yact[:, i, lq],
                                     i == 0, i == 10, B("psum", bk),
                                     reads=(B("W2T", sl), B("yact", i)))
                            k.vop("tensor_add", R[:, hc, ts], bank(bk), R[:, hc, ts],
                                  reads=(B("psum", bk), B("R", hc, tt)),
                                  writes=(B("R", hc, tt),))
        r.barrier()

    if "l1_attn" in phases:
        with ExitStack() as ph:
            xn = k.alloc("xn2", [128, NCHUNK, T], BF16, stack=ph)
            sq = k.alloc("sq2", [128, 2, TW], BF16, stack=ph)
            Oq = k.alloc("Oq2", [128, 4, T], BF16, stack=ph)
            Otmp = k.alloc("Otmp", [64, TW], BF16, stack=ph)
            QQ = k.alloc("QQ", [128, 2, TW], BF16, stack=ph)
            KK = k.alloc("KK2", [128, T], BF16, stack=ph)
            VV = k.alloc("VV2", [128, NCHUNK, 130], BF16, stack=ph)
            Pp = k.alloc("Pp2", [128, NCHUNK, TW], BF16, stack=ph)
            WS2 = k.alloc("WS2", [128, 4, NCHUNK, 128], BF16, stack=ph)
            WO2 = k.alloc("WO2", [128, 2, 4, 128], BF16, stack=ph)
            nc2 = k.alloc("nc2", [128, 2], F32, stack=ph)

            layernorm(xn, "xn2", sq, "sq2", "ln2")
            SC2 = 0.125

            k.vop("memset", VV[:, :, 64:65], 1.0, writes=(B("vvone1"),))
            k.vop("memset", VV[:, :, 129:130], 1.0, writes=(B("vvone2"),))

            for gp in range(4):
                g1, g2 = gp, gp + 4
                for (wsrc, s4, nm) in ((Wk, 0, "wk"), (Wv, 1, "wv")):
                    items = []
                    for ig, gg in enumerate((g1, g2)):
                        sap = bass.AP(wsrc.tensor, gg * 64,
                                      [[512, 128], [128 * 512, NCHUNK], [1, 64]])
                        items.append((WS2[:, s4, :, ig * 64:(ig + 1) * 64], sap, False))
                    k.dma_group("g", items, f"dws2_{s4}", writebufs=(B("WS2", s4),))
                for tt in range(NT):
                    bk = tt % 2
                    ts = slice(tt * TW, (tt + 1) * TW)
                    for ig, tp in ((0, 0), (1, 64)):
                        for j in range(NCHUNK):
                            k.mm(bankrow(bk, tp, tp + 64), WS2[:, 0, j, ig * 64:(ig + 1) * 64],
                                 xn[:, j, ts], j == 0, j == NCHUNK - 1, B("psum", bk),
                                 reads=(B("WS2", 0), B("xn2", j, tt)),
                                 tile_position=(0, tp))
                    k.vop("tensor_copy", KK[:, ts], bank(bk),
                          reads=(B("psum", bk),), writes=(B("KK2", tt),))
                for kc in range(NCHUNK):
                    bk = 2 + kc % 2
                    for ig in range(2):
                        oslice = PS[:, bk * TW + ig * 64:bk * TW + (ig + 1) * 64]
                        for j in range(NCHUNK):
                            k.mm(oslice, xn[:, j, kc * 128:(kc + 1) * 128],
                                 WS2[:, 1, j, ig * 64:(ig + 1) * 64],
                                 j == 0, j == NCHUNK - 1, B("psum", bk),
                                 reads=(B("WS2", 1), B("xn2", j, kc // 4)))
                    k.vop("tensor_copy", VV[:, kc, 0:64],
                          PS[:, bk * TW:bk * TW + 64],
                          reads=(B("psum", bk),), writes=(B("VV2", kc, 0),))
                    k.vop("tensor_copy", VV[:, kc, 65:129],
                          PS[:, bk * TW + 64:bk * TW + 128],
                          reads=(B("psum", bk),), writes=(B("VV2", kc, 1),))
                for ig, tp in ((0, 0), (1, 64)):
                    mx = []
                    for tt in range(NT):
                        ts = slice(tt * TW, (tt + 1) * TW)
                        sl = tt % 2
                        k.vop("tensor_mul", sq[tp:tp + 64, sl, :], KK[tp:tp + 64, ts],
                              KK[tp:tp + 64, ts], reads=(B("KK2", tt),),
                              writes=(B("sq2", sl),))
                        k.mm(bankrow(4, 0, 1), ones_col[tp:tp + 64, :],
                             sq[tp:tp + 64, sl, :], True, True, B("psum", 4),
                             reads=(B("ones_col"), B("sq2", sl)),
                             tile_position=(tp, 0))
                        k.vop("tensor_reduce", smallf[0:1, tt:tt + 1], bankrow(4, 0, 1),
                              AX.X, ALU.max, reads=(B("psum", 4),), writes=(B("kn", tt),))
                        mx.append(B("kn", tt))
                    k.vop("tensor_reduce", smallf[0:1, 16 + ig:17 + ig], smallf[0:1, 0:4],
                          AX.X, ALU.max, reads=tuple(mx), writes=(B("knm", ig),))
                for s4 in range(4):
                    c0 = (g1 if s4 < 2 else g2) * 256 + (s4 % 2) * 128
                    sap = bass.AP(Wq.tensor, c0, [[D, 128], [128 * D, NCHUNK], [1, 128]])
                    k.dma("g", WS2[:, s4, :, :], sap, f"dws2_{s4}", writes=(B("WS2", s4),))
                for hh in range(4):
                    for tt in range(NT):
                        ts = slice(tt * TW, (tt + 1) * TW)
                        for ig, tp in ((0, 0), (1, 64)):
                            gg = g1 if ig == 0 else g2
                            habs = 4 * gg + hh
                            fcL = (0 if ig == 0 else 2) + hh // 2
                            s4 = 2 * ig + hh // 2
                            mcol = (hh % 2) * 64
                            qsl = (hh * NT + tt) % 2
                            for j in range(NCHUNK):
                                k.mm(bankrow(7, tp, tp + 64), WS2[:, s4, j, mcol:mcol + 64],
                                     xn[:, j, ts], j == 0, j == NCHUNK - 1, B("psum", 7),
                                     reads=(B("WS2", s4), B("xn2", j, tt)),
                                     tile_position=(0, tp))
                            k.vop("tensor_copy", QQ[tp:tp + 64, qsl, :],
                                  bankrow(7, tp, tp + 64),
                                  reads=(B("psum", 7),), writes=(B("QQ", qsl, ig),))
                            k.vop("tensor_mul", sq[tp:tp + 64, 0, :], QQ[tp:tp + 64, qsl, :],
                                  QQ[tp:tp + 64, qsl, :],
                                  reads=(B("QQ", qsl, ig),), writes=(B("sq2", 0),))
                            k.mm(bankrow(4, 0, 1), ones_col[tp:tp + 64, :],
                                 sq[tp:tp + 64, 0, :], True, True, B("psum", 4),
                                 reads=(B("ones_col"), B("sq2", 0)),
                                 tile_position=(tp, 0))
                            k.vop("tensor_reduce", smallf[0:1, 20:21], bankrow(4, 0, 1),
                                  AX.X, ALU.max, reads=(B("psum", 4),), writes=(B("qn"),))
                            k.vop("tensor_add", smallf[0:1, 21:22], smallf[0:1, 20:21],
                                  smallf[0:1, 16 + ig:17 + ig],
                                  reads=(B("qn"), B("knm", ig)), writes=(B("cs"),))
                            k.vop("tensor_scalar", smallf[0:1, 22:23], smallf[0:1, 21:22],
                                  -0.5 * SC2, -4.0, ALU.mult, ALU.add,
                                  reads=(B("cs"),), writes=(B("csn"),))
                            k.mm(PS[:, 4 * TW + 64:4 * TW + 65], ones_row[:],
                                 smallf[0:1, 22:23], True, True, B("psum", 4),
                                 reads=(B("ones_row"), B("csn")))
                            k.act(nc2[:, ig:ig + 1], PS[:, 4 * TW + 64:4 * TW + 65],
                                  AF.Copy, reads=(B("psum", 4),), writes=(B("nc2", ig),))
                            for j in range(NCHUNK):
                                bk = (j % 2) + 2 * ig
                                k.mm(bank(bk), KK[tp:tp + 64, j * 128:(j + 1) * 128],
                                     QQ[tp:tp + 64, qsl, :], True, True, B("psum", bk),
                                     reads=(B("KK2", j // 4), B("QQ", qsl, ig)),
                                     tile_position=(tp, 0))
                                k.act(Pp[:, j, :], bank(bk), AF.Exp,
                                      bias=nc2[:, ig:ig + 1], scale=SC2,
                                      reads=(B("psum", bk), B("nc2", ig)),
                                      writes=(B("Pp2", j),))
                            ob = 5 + ig
                            for j in range(NCHUNK):
                                k.mm(PS[0:65, ob * TW:ob * TW + TW],
                                     VV[:, j, ig * 65:(ig + 1) * 65], Pp[:, j, :],
                                     j == 0, j == NCHUNK - 1, B("psum", ob),
                                     reads=(B("VV2", j, ig), B("vvone1"), B("vvone2"),
                                            B("Pp2", j)))
                            k.vop("reciprocal", zr[:], PS[64:65, ob * TW:ob * TW + TW],
                                  reads=(B("psum", ob),), writes=(B("zr"),))
                            k.mm(bankrow(4, 0, 64), ones_row[:, 0:64], zr[:], True, True,
                                 B("psum", 4), reads=(B("ones_row"), B("zr")))
                            k.act(zb[0:64, :], bankrow(4, 0, 64), AF.Copy,
                                  reads=(B("psum", 4),), writes=(B("zb"),))
                            if habs % 2 == 0:
                                k.vop("tensor_mul", Oq[0:64, fcL, ts],
                                      PS[0:64, ob * TW:ob * TW + TW], zb[0:64, :],
                                      reads=(B("psum", ob), B("zb")),
                                      writes=(B("Oq2", fcL, 0, tt),))
                            else:
                                k.vop("tensor_mul", Otmp[:],
                                      PS[0:64, ob * TW:ob * TW + TW], zb[0:64, :],
                                      reads=(B("psum", ob), B("zb")),
                                      writes=(B("Otmp"),))
                                k.dma("y", Oq[64:128, fcL, ts], Otmp[:], "dOq",
                                      reads=(B("Otmp"),), writes=(B("Oq2", fcL, 1, tt),))
                for jd in range(NCHUNK):
                    wsl = jd % 2
                    items = []
                    for half in range(2):
                        rbase = (256 * gp if half == 0 else 1024 + 256 * gp)
                        sap = bass.AP(Wo.tensor, rbase * D + jd * 128,
                                      [[D, 128], [128 * D, 2], [1, 128]])
                        items.append((WO2[:, wsl, 2 * half:2 * half + 2, :], sap, False))
                    k.dma_group("g", items, f"dwo2_{wsl}", writebufs=(B("WO2", wsl),))
                    for tt in range(NT):
                        bk = tt % 2
                        ts = slice(tt * TW, (tt + 1) * TW)
                        for fc in range(4):
                            k.mm(bank(bk), WO2[:, wsl, fc, :], Oq[:, fc, ts],
                                 fc == 0, fc == 3, B("psum", bk),
                                 reads=(B("WO2", wsl), B("Oq2", fc, 0, tt),
                                        B("Oq2", fc, 1, tt)))
                        k.vop("tensor_add", R[:, jd, ts], bank(bk), R[:, jd, ts],
                              reads=(B("psum", bk), B("R", jd, tt)),
                              writes=(B("R", jd, tt),))
        r.barrier()

    if "l1_ffn" in phases:
        with ExitStack() as ph:
            xnq = k.alloc("xnq", [128, NCHUNK, TW], BF16, stack=ph)
            sq = k.alloc("sq3", [128, 2, TW], BF16, stack=ph)
            hq = k.alloc("hq", [128, 64, TW], BF16, stack=ph)
            WGT = k.alloc("WGT", [128, 3, NCHUNK, 128], BF16, stack=ph)
            WPT = k.alloc("WPT", [128, 2, 32, 128], BF16, stack=ph)
            QS = k.alloc("QS", [128, 2, D], BF16, stack=ph)
            QT = k.alloc("QT", [128, 2, NCHUNK, 128], BF16, stack=ph)
            qa = k.alloc("qa", [128, 64], F32, stack=ph)
            qsb = k.alloc("qsb", [128, 64], BF16, stack=ph)
            qsf = k.alloc("qsf", [128, 64], F32, stack=ph)
            qr = k.alloc("qr", [128, 64], F32, stack=ph)
            zbq = zb

            def quant_block(src_ap, ng, slot, outbuf, store_dst):
                S = QS[:, slot, :]
                S3 = QS[:, slot, :].rearrange("p (g x) -> p g x", x=64)
                A = qa[:, 0:ng]
                SB = qsb[:, 0:ng]
                SF = qsf[:, 0:ng]
                RC = qr[:, 0:ng]
                k.dma("g", S, src_ap, f"dQS{slot}", writes=(B("QS", slot),))
                k.vop("tensor_reduce", A, S3, AX.X, ALU.add, apply_absolute_value=True,
                      reads=(B("QS", slot),), writes=(B("qa", slot),))
                k.vop("tensor_scalar", SB, A, 1.0 / 64.0, 1e-8, ALU.mult, ALU.max,
                      reads=(B("qa", slot),), writes=(B("qsb", slot),))
                k.vop("tensor_copy", SF, SB, reads=(B("qsb", slot),), writes=(B("qsf", slot),))
                k.vop("reciprocal", RC, SF, reads=(B("qsf", slot),), writes=(B("qr", slot),))
                rb = RC.unsqueeze(2).broadcast_to([128, ng, 64])
                sb = SF.unsqueeze(2).broadcast_to([128, ng, 64])
                k.vop("tensor_tensor", S3, S3, rb, ALU.mult,
                      reads=(B("QS", slot), B("qr", slot)), writes=(B("QS", slot),))
                k.vop("tensor_scalar", S, S, MAGIC, -MAGIC, ALU.add, ALU.add,
                      reads=(B("QS", slot),), writes=(B("QS", slot),))
                k.vop("tensor_scalar", S, S, 1.0, -1.0, ALU.min, ALU.max,
                      reads=(B("QS", slot),), writes=(B("QS", slot),))
                k.vop("tensor_tensor", S3, S3, sb, ALU.mult,
                      reads=(B("QS", slot), B("qsf", slot)), writes=(B("QS", slot),))
                k.dma_group("y",
                            [(QT[:, slot, j, :], S[j * 128:(j + 1) * 128] if False else QS[:, slot, j * 128:(j + 1) * 128], True)
                             for j in range(NCHUNK)],
                            f"dQT{slot}", reads=(B("QS", slot),),
                            writebufs=(B("QT", slot),))
                k.dma("g", store_dst, QT[:, slot, :, :], f"dQW{slot}",
                      reads=(B("QT", slot),), writes=(outbuf,))

            def q_wgu(rb):
                sap = bass.AP(wgu.tensor, rb * 128 * D, [[D, 128], [1, D]])
                dst = bass.AP(sc_wgu.tensor, rb * D * 128,
                              [[128, 128], [128 * 128, NCHUNK], [1, 128]])
                quant_block(sap, 32, rb % 2, B("scwgu", rb), dst)

            def q_wp(dc, hx):
                sap = bass.AP(wp.tensor, dc * 128 * 8192 + hx * D, [[8192, 128], [1, D]])
                dst = bass.AP(sc_wp.tensor, dc * 8192 * 128 + hx * D * 128,
                              [[128, 128], [128 * 128, NCHUNK], [1, 128]])
                quant_block(sap, 32, (dc * 4 + hx) % 2, B("scwp", dc, hx), dst)

            for qt in range(NT):
                ts = slice(qt * TW, (qt + 1) * TW)
                layernorm(xnq, "xnq", sq, "sq3", f"ln3_{qt}", tts=[qt], toff=qt)
                for c in range(64):
                    if qt == 0:
                        q_wgu(c)
                        q_wgu(64 + c)
                    slG = (2 * c) % 3
                    slU = (2 * c + 1) % 3
                    k.dma("y", WGT[:, slG, :, :],
                          bass.AP(sc_wgu.tensor, c * D * 128, [[128, 128], [128 * 128, NCHUNK], [1, 128]]),
                          f"dWGT{slG}", reads=(B("scwgu", c),), writes=(B("WGT", slG),))
                    k.dma("y", WGT[:, slU, :, :],
                          bass.AP(sc_wgu.tensor, (64 + c) * D * 128, [[128, 128], [128 * 128, NCHUNK], [1, 128]]),
                          f"dWGT{slU}", reads=(B("scwgu", 64 + c),), writes=(B("WGT", slU),))
                    bkG = (c % 2) * 2
                    bkU = bkG + 1
                    for j in range(NCHUNK):
                        k.mm(bank(bkG), WGT[:, slG, j, :], xnq[:, j, :],
                             j == 0, j == NCHUNK - 1, B("psum", bkG),
                             reads=(B("WGT", slG), B("xnq", j, qt)))
                    for j in range(NCHUNK):
                        k.mm(bank(bkU), WGT[:, slU, j, :], xnq[:, j, :],
                             j == 0, j == NCHUNK - 1, B("psum", bkU),
                             reads=(B("WGT", slU), B("xnq", j, qt)))
                    k.act(sq[:, 0, :], bank(bkG), AF.Silu,
                          reads=(B("psum", bkG),), writes=(B("sq3", 0),))
                    k.vop("tensor_mul", hq[:, c, :], sq[:, 0, :], bank(bkU),
                          reads=(B("sq3", 0), B("psum", bkU)), writes=(B("hq", c),))
                    k.vop("tensor_mul", sq[:, 1, :], hq[:, c, :], hq[:, c, :],
                          reads=(B("hq", c),), writes=(B("sq3", 1),))
                    k.mm(bankrow(4, 0, 1), ones_col[:], sq[:, 1, :], c == 0, c == 63,
                         B("psum", 4), reads=(B("ones_col"), B("sq3", 1)))
                rsqrt_rows(f"rq{qt}", bankrow(4, 0, 1), B("psum", 4), 1.0 / 8192.0, RMS_EPS)
                k.mm(bank(5), ones_row[:], rows[:, 4, :], True, True, B("psum", 5),
                     reads=(B("ones_row"), B(f"rq{qt}", "m1")))
                k.act(zbq[:], bank(5), AF.Copy, reads=(B("psum", 5),), writes=(B("zbq"),))
                for dc in range(NCHUNK):
                    if qt == 0:
                        for hx in range(4):
                            q_wp(dc, hx)
                    for half in range(2):
                        k.dma("y", WPT[:, half, :, :],
                              bass.AP(sc_wp.tensor, dc * 8192 * 128 + half * 4096 * 128,
                                      [[128, 128], [128 * 128, 32], [1, 128]]),
                              f"dWPT{half}",
                              reads=(B("scwp", dc, 2 * half), B("scwp", dc, 2 * half + 1)),
                              writes=(B("WPT", half),))
                    bk = 6 + dc % 2
                    for half in range(2):
                        for i in range(32):
                            hi = half * 32 + i
                            k.mm(bank(bk), WPT[:, half, i, :], hq[:, hi, :],
                                 hi == 0, hi == 63, B("psum", bk),
                                 reads=(B("WPT", half), B("hq", hi)))
                    k.vop("tensor_mul", sq[:, 1, :], bank(bk), zbq[:],
                          reads=(B("psum", bk), B("zbq")), writes=(B("sq3", 1),))
                    k.vop("tensor_add", R[:, dc, ts], sq[:, 1, :], R[:, dc, ts],
                          reads=(B("sq3", 1), B("R", dc, qt)),
                          writes=(B("R", dc, qt),))
        r.barrier()

    with ExitStack() as ph:
        stg = k.alloc("stg_out", [128, 2, D], BF16, stack=ph)
        for tt in range(16):
            sl = tt % 2
            stgap = stg[:, sl, :]
            k.dma_group("y",
                        [(stgap[:, j * 128:(j + 1) * 128], R[:, j, tt * 128:(tt + 1) * 128], True)
                         for j in range(NCHUNK)],
                        f"dOstg{sl}",
                        reads=tuple(B("R", j, tt // 4) for j in range(NCHUNK)),
                        writebufs=(B("stg_out", sl),))
            k.dma("g", out[tt * 128:(tt + 1) * 128, :], stgap,
                  f"dOut{sl}", reads=(B("stg_out", sl),))

        def final(eng):
            eng.wait_ge(r.sems["dOut0"], r.cnt["dOut0"])
            eng.wait_ge(r.sems["dOut1"], r.cnt["dOut1"])
            return None
        r.sem("dOut0")
        r.sem("dOut1")
        r.ops["y"].append(((), final, "Ey", 0))
        r.sem("Ey")

    with es, nc.Block() as block:
        @block.tensor
        def _(eng):
            r.replay("t", eng)

        @block.vector
        def _(eng):
            r.replay("v", eng)

        @block.scalar
        def _(eng):
            r.replay("s", eng)

        @block.gpsimd
        def _(eng):
            r.replay("g", eng)

        @block.sync
        def _(eng):
            r.replay("y", eng)

    return nc


def kernel(L0_attn_x, L0_attn_w_qkv, L0_attn_b_qkv, L0_attn_w_out, L0_attn_b_out,
           L0_ffn_router_w, L0_ffn_w1, L0_ffn_w2,
           L1_attn_Wq, L1_attn_Wk, L1_attn_Wv, L1_attn_Wo,
           L1_ffn_w_gate_up, L1_ffn_w_proj):
    nc = build()
    common = {
        "w_qkv": np.ascontiguousarray(L0_attn_w_qkv, dtype=np.float32),
        "b_qkv": np.ascontiguousarray(L0_attn_b_qkv, dtype=np.float32),
        "w_out": np.ascontiguousarray(L0_attn_w_out, dtype=np.float32),
        "b_out": np.ascontiguousarray(L0_attn_b_out, dtype=np.float32),
        "router_w": np.ascontiguousarray(L0_ffn_router_w, dtype=np.float32),
        "w1": np.ascontiguousarray(L0_ffn_w1, dtype=np.float32),
        "w2": np.ascontiguousarray(L0_ffn_w2, dtype=np.float32),
        "Wq": np.ascontiguousarray(L1_attn_Wq, dtype=np.float32),
        "Wk": np.ascontiguousarray(L1_attn_Wk, dtype=np.float32),
        "Wv": np.ascontiguousarray(L1_attn_Wv, dtype=np.float32),
        "Wo": np.ascontiguousarray(L1_attn_Wo, dtype=np.float32),
        "wgu": np.ascontiguousarray(L1_ffn_w_gate_up, dtype=np.float32),
        "wp": np.ascontiguousarray(L1_ffn_w_proj, dtype=np.float32),
    }
    xs = np.ascontiguousarray(np.asarray(L0_attn_x, dtype=np.float32))
    in_maps = [dict(common, x=np.ascontiguousarray(xs[c])) for c in range(8)]
    res = run_bass_kernel_spmd(nc, in_maps, core_ids=list(range(8)))
    return np.stack([np.asarray(rr["out"], dtype=np.float32) for rr in res.results], axis=0)


# version 04
# speedup vs baseline: 1.8914x; 1.8914x over previous
import numpy as np
import ml_dtypes
from contextlib import ExitStack
import concourse.bass as bass
import concourse.mybir as mybir
from concourse.bass_utils import run_bass_kernel_spmd

F32 = mybir.dt.float32
BF16 = mybir.dt.bfloat16
AF = mybir.ActivationFunctionType
ALU = mybir.AluOpType
AX = mybir.AxisListType

D = 2048
T = 2048
NCHUNK = 16
NT = 4
TW = 512
MAGIC = float(np.float32(2.0 ** 23 + 2.0 ** 22))
LN_EPS = 1e-5
RMS_EPS = float(np.finfo(np.float32).eps)


class Buf:
    __slots__ = ("w", "r")

    def __init__(self):
        self.w = None
        self.r = []


class Rec:

    def __init__(self, nc, es):
        self.nc = nc
        self.es = es
        self.ops = {e: [] for e in "tvsgy"}
        self.sems = {}
        self.cnt = {}
        self.lastwait = {}
        self.bufs = {}
        self.dma_slots = set()

    def B(self, *key):
        if key not in self.bufs:
            self.bufs[key] = Buf()
        return self.bufs[key]

    def sem(self, name):
        if name not in self.sems:
            self.sems[name] = self.es.enter_context(self.nc.semaphore(name))
            self.cnt[name] = 0
        return self.sems[name]

    def op(self, e, emit, reads=(), writes=(), deps=(), sem=None, inc=1, waw=True):
        semname = sem if sem is not None else ("E" + e)
        self.sem(semname)
        if inc == 16:
            self.dma_slots.add(semname)
        dl = []
        for b in reads:
            dl.append(b.w)
        for b in writes:
            if waw:
                dl.append(b.w)
            dl.extend(b.r)
        dl.extend(deps)
        waits = []
        for d in dl:
            if d is None:
                continue
            sn, val = d
            if e == "t" and sn == "Et":
                continue
            key = (e, sn)
            if self.lastwait.get(key, 0) >= val:
                continue
            self.lastwait[key] = val
            waits.append((sn, val))
        self.cnt[semname] += inc
        tok = (semname, self.cnt[semname])
        self.ops[e].append((waits, emit, semname, inc))
        for b in reads:
            b.r.append(tok)
        for b in writes:
            b.w = tok
            b.r = []
        return tok

    def barrier(self):
        targets = [(sn, self.cnt[sn]) for sn in list(self.cnt)
                   if self.cnt[sn] > 0]
        for e in "tvsgy":
            def emit(eng):
                return eng.nop()
            self.op(e, emit, deps=targets)

    def replay(self, e, eng):
        for waits, emit, semname, inc in self.ops[e]:
            for sn, val in waits:
                eng.wait_ge(self.sems[sn], val)
            ins = emit(eng)
            if ins is not None:
                ins.then_inc(self.sems[semname], inc)


class K:
    def __init__(self, nc, es):
        self.nc = nc
        self.es = es
        self.r = Rec(nc, es)

    def alloc(self, name, shape, dtype, stack=None):
        return (stack or self.es).enter_context(self.nc.sbuf_tensor(name, shape, dtype))

    def do(self, e, fn, **kw):
        return self.r.op(e, fn, **kw)

    def dma(self, e, dst_ap, src_ap, slot, reads=(), writes=(), deps=(), transpose=False, waw=True, slow=False):
        def emit(eng):
            if slow:
                return eng.dma_start(dst_ap, src_ap, transpose=transpose,
                                     allow_slow_non_contiguous=True)
            return eng.dma_start(dst_ap, src_ap, transpose=transpose)
        return self.r.op(e, emit, reads=reads, writes=writes, deps=deps, sem=slot,
                         inc=16, waw=waw)

    def dma_group(self, e, items, slot, reads=(), writebufs=(), deps=()):
        tok = None
        first = True
        for dst_ap, src_ap, tr in items:
            def emit(eng, d=dst_ap, s=src_ap, t=tr):
                return eng.dma_start(d, s, transpose=t)
            tok = self.r.op(e, emit, reads=reads if first else (),
                            writes=writebufs if first else (), deps=deps if first else (),
                            sem=slot, inc=16, waw=first)
            first = False
        for b in writebufs:
            b.w = tok
            b.r = []
        for b in reads:
            b.r.append(tok)
        return tok

    def mm(self, out_ap, lhsT_ap, rhs_ap, start, stop, bankbuf, reads=(), deps=(), tile_position=None):
        def emit(eng):
            return eng.matmul(out_ap, lhsT_ap, rhs_ap, start=start, stop=stop,
                              tile_position=tile_position)
        return self.r.op("t", emit, reads=reads, writes=(bankbuf,), deps=deps)

    def act(self, out_ap, in_ap, func, reads=(), writes=(), deps=(), bias=0.0, scale=1.0):
        def emit(eng):
            return eng.activation(out_ap, in_ap, func, bias=bias, scale=scale)
        return self.r.op("s", emit, reads=reads, writes=writes, deps=deps)

    def vop(self, name, *args, reads=(), writes=(), deps=(), engine="v", **kw):
        def emit(eng):
            return getattr(eng, name)(*args, **kw)
        return self.r.op(engine, emit, reads=reads, writes=writes, deps=deps)


def build(phases=("l0_attn", "l0_moe", "l1_attn", "l1_ffn"), nc_cls=None):
    if nc_cls is None:
        nc_cls = bass.Bass
    nc = nc_cls("TRN2", target_bir_lowering=False)

    x = nc.dram_tensor("x", [T, D], F32, kind="ExternalInput").ap()
    w_qkv = nc.dram_tensor("w_qkv", [D, 3 * D], F32, kind="ExternalInput").ap()
    b_qkv = nc.dram_tensor("b_qkv", [3 * D], F32, kind="ExternalInput").ap()
    w_out = nc.dram_tensor("w_out", [D, D], F32, kind="ExternalInput").ap()
    b_out = nc.dram_tensor("b_out", [D], F32, kind="ExternalInput").ap()
    router_w = nc.dram_tensor("router_w", [8, D], F32, kind="ExternalInput").ap()
    w1 = nc.dram_tensor("w1", [8, 2816, D], F32, kind="ExternalInput").ap()
    w2 = nc.dram_tensor("w2", [8, D, 1408], F32, kind="ExternalInput").ap()
    Wq = nc.dram_tensor("Wq", [D, D], F32, kind="ExternalInput").ap()
    Wk = nc.dram_tensor("Wk", [D, 512], F32, kind="ExternalInput").ap()
    Wv = nc.dram_tensor("Wv", [D, 512], F32, kind="ExternalInput").ap()
    Wo = nc.dram_tensor("Wo", [D, D], F32, kind="ExternalInput").ap()
    wgu = nc.dram_tensor("wgu", [16384, D], F32, kind="ExternalInput").ap()
    wp = nc.dram_tensor("wp", [D, 8192], F32, kind="ExternalInput").ap()
    out = nc.dram_tensor("out", [T, D], F32, kind="ExternalOutput").ap()
    sc_wgu = nc.dram_tensor("sc_wgu", [128, D, 128], BF16, kind="Internal").ap()
    sc_wp = nc.dram_tensor("sc_wp", [16, 8192, 128], BF16, kind="Internal").ap()

    es = ExitStack()
    k = K(nc, es)
    r = k.r
    B = r.B

    R = k.alloc("R", [128, NCHUNK, T], BF16)
    ones_col = k.alloc("ones_col", [128, 1], BF16)
    ones_row = k.alloc("ones_row", [1, 128], F32)
    rows = k.alloc("rows", [1, 5, TW], F32)
    smallf = k.alloc("smallf", [128, 32], F32)
    zr = k.alloc("zr", [1, TW], F32)
    zb = k.alloc("zb", [128, TW], BF16)
    negc = k.alloc("negc", [128, 1], F32)
    PS = es.enter_context(nc.psum_tensor("PS", [128, 4096], F32))

    def bank(i, w=TW):
        return PS[:, i * TW:i * TW + w]

    def bankrow(i, p0, p1, w=TW):
        return PS[p0:p1, i * TW:i * TW + w]

    k.vop("memset", ones_col[:], 1.0, writes=(B("ones_col"),))
    k.vop("memset", ones_row[:], 1.0, writes=(B("ones_row"),))
    CONSTS = [LN_EPS, RMS_EPS, 1.0 / float(np.sqrt(128.0)), 0.125, 1.0, 0.0]
    ctile = k.alloc("ctile", [128, len(CONSTS)], F32)
    for ci, cv in enumerate(CONSTS):
        k.vop("memset", ctile[:, ci:ci + 1], float(cv), writes=(B("ctile"),))
        nc.const_aps.register(F32, float(cv), ctile[:, ci:ci + 1])
    r.barrier()

    def rsqrt_rows(tag, src_psum, src_buf, scale, eps):
        S2 = rows[:, 1, :]
        MU = rows[:, 2, :]
        TMP = rows[:, 3, :]
        M1 = rows[:, 4, :]
        k.vop("tensor_scalar_mul", S2, src_psum, scale,
              reads=(src_buf,), writes=(B(tag, "s2"),))
        k.act(TMP, S2, AF.Ln, bias=eps, reads=(B(tag, "s2"),), writes=(B(tag, "tmp"),))
        k.act(M1, TMP, AF.Exp, scale=-0.5, reads=(B(tag, "tmp"),), writes=(B(tag, "m1"),))

    def layernorm(xn, xnname, sq, sqname, tag, eps=LN_EPS, tts=None, toff=0):
        S1 = rows[:, 0, :]
        S2 = rows[:, 1, :]
        MU = rows[:, 2, :]
        TMP = rows[:, 3, :]
        M1 = rows[:, 4, :]
        M0 = rows[:, 1, :]
        if tts is None:
            tts = range(NT)
        for tt in tts:
            ts = slice(tt * TW, (tt + 1) * TW)
            xts = slice((tt - toff) * TW, (tt - toff + 1) * TW)
            for j in range(NCHUNK):
                k.mm(bankrow(0, 0, 1), ones_col[:], R[:, j, ts], j == 0, j == NCHUNK - 1,
                     B("psum", 0), reads=(B("ones_col"), B("R", j, tt)))
            for j in range(NCHUNK):
                sl = j % 2
                k.vop("tensor_mul", sq[:, sl, :], R[:, j, ts], R[:, j, ts],
                      reads=(B("R", j, tt),), writes=(B(sqname, sl),))
                k.mm(bankrow(1, 0, 1), ones_col[:], sq[:, sl, :], j == 0, j == NCHUNK - 1,
                     B("psum", 1), reads=(B("ones_col"), B(sqname, sl)))
            k.vop("tensor_scalar_mul", S1, bankrow(0, 0, 1), 1.0 / D,
                  reads=(B("psum", 0),), writes=(B(tag, "s1"),))
            k.vop("tensor_scalar_mul", S2, bankrow(1, 0, 1), 1.0 / D,
                  reads=(B("psum", 1),), writes=(B(tag, "s2"),))
            k.vop("tensor_mul", MU, S1, S1, reads=(B(tag, "s1"),), writes=(B(tag, "mu"),))
            k.vop("tensor_sub", S2, S2, MU, reads=(B(tag, "s2"), B(tag, "mu")),
                  writes=(B(tag, "s2"),))
            k.act(TMP, S2, AF.Ln, bias=eps, reads=(B(tag, "s2"),), writes=(B(tag, "tmp"),))
            k.act(M1, TMP, AF.Exp, scale=-0.5, reads=(B(tag, "tmp"),), writes=(B(tag, "m1"),))
            k.vop("tensor_mul", M0, S1, M1, reads=(B(tag, "s1"), B(tag, "m1")),
                  writes=(B(tag, "s2"),))
            k.vop("tensor_scalar_mul", M0, M0, -1.0, reads=(B(tag, "s2"),),
                  writes=(B(tag, "s2"),))
            bcb = 2 + 2 * (tt % 3)
            k.mm(bank(bcb), ones_row[:], M1, True, True, B("psum", bcb),
                 reads=(B("ones_row"), B(tag, "m1")))
            k.mm(bank(bcb + 1), ones_row[:], M0, True, True, B("psum", bcb + 1),
                 reads=(B("ones_row"), B(tag, "s2")))
            for j in range(NCHUNK):
                sl = j % 2
                k.vop("tensor_mul", sq[:, sl, :], R[:, j, ts], bank(bcb),
                      reads=(B("R", j, tt), B("psum", bcb)), writes=(B(sqname, sl),))
                k.vop("tensor_add", xn[:, j, xts], sq[:, sl, :], bank(bcb + 1),
                      reads=(B(sqname, sl), B("psum", bcb + 1)), writes=(B(xnname, j, tt),))

    with ExitStack() as ph:
        stg = k.alloc("stg_in", [128, 1, D], BF16, stack=ph)
        for tt in range(16):
            sl = 0
            stgap = stg[:, sl, :]
            k.dma("g", stgap, x[tt * 128:(tt + 1) * 128, :], f"dstg{sl}",
                  writes=(B("stg_in", sl),))
            k.dma_group("y",
                        [(R[:, j, tt * 128:(tt + 1) * 128], stgap[:, j * 128:(j + 1) * 128], True)
                         for j in range(NCHUNK)],
                        f"dRfill{sl}", reads=(B("stg_in", sl),),
                        writebufs=tuple(B("R", j, tt // 4) for j in range(NCHUNK)))

        if "l0_attn" in phases:
            xn = k.alloc("xn0", [128, NCHUNK, T], BF16, stack=ph)
            sq = k.alloc("sq0", [128, 2, TW], BF16, stack=ph)
            Qh = k.alloc("Qh", [128, T], BF16, stack=ph)
            Kh = k.alloc("Kh", [128, T], BF16, stack=ph)
            Vtm = k.alloc("Vtm", [128, NCHUNK, 128], BF16, stack=ph)
            Pp = k.alloc("Pp", [128, NCHUNK, TW], BF16, stack=ph)
            Oq = k.alloc("Oq", [128, 4, T], BF16, stack=ph)
            WS = k.alloc("WS", [128, 3, NCHUNK, 128], BF16, stack=ph)
            WO = k.alloc("WO", [128, 2, 4, 128], BF16, stack=ph)
            biasq = k.alloc("biasq", [128, 4], F32, stack=ph)
            bo = k.alloc("bo", [128, NCHUNK], F32, stack=ph)

            layernorm(xn, "xn0", sq, "sq0", "ln0")
            k.dma("g", bo[:], bass.AP(b_out.tensor, 0, [[1, 128], [128, NCHUNK]]),
                  "dbo", writes=(B("bo"),), slow=True)

            SC = 1.0 / float(np.sqrt(128.0))

            def load_qkv_slice(h, wi):
                sl = (h * 3 + wi) % 3
                c0 = wi * D + h * 128
                src = bass.AP(w_qkv.tensor, c0, [[3 * D, 128], [128 * 3 * D, NCHUNK], [1, 128]])
                k.dma("g", WS[:, sl, :, :], src, f"dws{sl}", writes=(B("WS", sl),))
                return sl

            for h in range(16):
                slq = load_qkv_slice(h, 0)
                slk = load_qkv_slice(h, 1)
                slv = load_qkv_slice(h, 2)
                k.dma("g", biasq[:, 0:3],
                      bass.AP(b_qkv.tensor, h * 128, [[1, 128], [D, 3]]),
                      "dbq", writes=(B("biasq"),), slow=True)

                for (dst, dstn, wsl, wi, bki) in ((Qh, "Qh", slq, 0, 0), (Kh, "Kh", slk, 1, 1)):
                    for tt in range(NT):
                        for j in range(NCHUNK):
                            k.mm(bank(bki), WS[:, wsl, j, :], xn[:, j, tt * TW:(tt + 1) * TW],
                                 j == 0, j == NCHUNK - 1, B("psum", bki),
                                 reads=(B("WS", wsl), B("xn0", j, tt)))
                        k.act(dst[:, tt * TW:(tt + 1) * TW], bank(bki), AF.Identity,
                              bias=biasq[:, wi:wi + 1],
                              reads=(B("psum", bki), B("biasq")), writes=(B(dstn, tt),))
                mx = []
                for (src, srcn) in ((Qh, "Qh"), (Kh, "Kh")):
                    for tt in range(NT):
                        ts = slice(tt * TW, (tt + 1) * TW)
                        sl = tt % 2
                        k.vop("tensor_mul", sq[:, sl, :], src[:, ts], src[:, ts],
                              reads=(B(srcn, tt),), writes=(B("sq0", sl),))
                        k.mm(bankrow(4, 0, 1), ones_col[:], sq[:, sl, :],
                             True, True, B("psum", 4), reads=(B("ones_col"), B("sq0", sl)))
                        ix = len(mx)
                        k.vop("tensor_reduce", smallf[0:1, ix:ix + 1], bankrow(4, 0, 1),
                              AX.X, ALU.max, reads=(B("psum", 4),), writes=(B("smx", ix),))
                        mx.append(B("smx", ix))
                k.vop("tensor_reduce", smallf[0:1, 8:9], smallf[0:1, 0:8], AX.X, ALU.max,
                      reads=tuple(mx), writes=(B("cmax"),))
                k.vop("tensor_scalar", smallf[0:1, 9:10], smallf[0:1, 8:9], -SC, -4.0,
                      ALU.mult, ALU.add, reads=(B("cmax"),), writes=(B("cneg"),))
                k.mm(PS[:, 4 * TW:4 * TW + 1], ones_row[:], smallf[0:1, 9:10], True, True,
                     B("psum", 4), reads=(B("ones_row"), B("cneg")))
                k.act(negc[:], PS[:, 4 * TW:4 * TW + 1], AF.Copy,
                      reads=(B("psum", 4),), writes=(B("negc"),))

                for kc in range(NCHUNK):
                    bk = 2 + kc % 2
                    for j in range(NCHUNK):
                        k.mm(bank(bk, 128), xn[:, j, kc * 128:(kc + 1) * 128], WS[:, slv, j, :],
                             j == 0, j == NCHUNK - 1, B("psum", bk),
                             reads=(B("WS", slv), B("xn0", j, kc // 4)))
                    k.act(Vtm[:, kc, :], bank(bk, 128), AF.Identity, bias=biasq[:, 2:3],
                          reads=(B("psum", bk), B("biasq")), writes=(B("Vtm", kc),))

                hq = h % 4
                for tt in range(NT):
                    ts = slice(tt * TW, (tt + 1) * TW)

                    def zo(j):
                        k.mm(bankrow(5, 0, 1), ones_col[:], Pp[:, j, :], j == 0, j == NCHUNK - 1,
                             B("psum", 5), reads=(B("ones_col"), B("Pp", j)))
                        k.mm(bank(6), Vtm[:, j, :], Pp[:, j, :], j == 0, j == NCHUNK - 1,
                             B("psum", 6), reads=(B("Vtm", j), B("Pp", j)))
                    for j in range(NCHUNK):
                        bk = j % 4
                        k.mm(bank(bk), Kh[:, j * 128:(j + 1) * 128], Qh[:, ts], True, True,
                             B("psum", bk), reads=(B("Kh", j // 4), B("Qh", tt)))
                        if j % 2 == 1:
                            b0 = bk - 1
                            k.act(Pp[:, j - 1:j + 1, :].rearrange("p a b -> p (a b)"),
                                  PS[:, b0 * TW:(b0 + 2) * TW], AF.Exp,
                                  bias=negc[:], scale=SC,
                                  reads=(B("psum", b0), B("psum", bk), B("negc")),
                                  writes=(B("Pp", j - 1), B("Pp", j)))
                            if j >= 3:
                                zo(j - 3)
                                zo(j - 2)
                    for j in (14, 15):
                        zo(j)
                    k.vop("reciprocal", zr[:], bankrow(5, 0, 1),
                          reads=(B("psum", 5),), writes=(B("zr"),))
                    k.mm(bank(7), ones_row[:], zr[:], True, True, B("psum", 7),
                         reads=(B("ones_row"), B("zr")))
                    k.act(zb[:], bank(7), AF.Copy, reads=(B("psum", 7),), writes=(B("zb"),))
                    k.vop("tensor_mul", Oq[:, hq, ts], bank(6), zb[:],
                          reads=(B("psum", 6), B("zb")), writes=(B("Oq", hq, tt),))

                if hq == 3:
                    qq = h // 4
                    for jd in range(NCHUNK):
                        wsl = jd % 2
                        src = bass.AP(w_out.tensor, (qq * 4 * 128) * D + jd * 128,
                                      [[D, 128], [128 * D, 4], [1, 128]])
                        k.dma("g", WO[:, wsl, :, :], src, f"dwo{wsl}", writes=(B("WO", wsl),))
                        for tt in range(NT):
                            bk = 5 + tt % 2
                            ts = slice(tt * TW, (tt + 1) * TW)
                            for fc in range(4):
                                k.mm(bank(bk), WO[:, wsl, fc, :], Oq[:, fc, ts],
                                     fc == 0, fc == 3, B("psum", bk),
                                     reads=(B("WO", wsl), B("Oq", fc, tt)))
                            if qq == 0:
                                k.vop("scalar_tensor_tensor", R[:, jd, ts], bank(bk),
                                      bo[:, jd:jd + 1], R[:, jd, ts], ALU.add, ALU.add,
                                      reads=(B("psum", bk), B("bo"), B("R", jd, tt)),
                                      writes=(B("R", jd, tt),))
                            else:
                                k.vop("tensor_add", R[:, jd, ts], bank(bk), R[:, jd, ts],
                                      reads=(B("psum", bk), B("R", jd, tt)),
                                      writes=(B("R", jd, tt),))
        r.barrier()

    if "l0_moe" in phases:
        with ExitStack() as ph:
            xn = k.alloc("xn1", [128, NCHUNK, T], BF16, stack=ph)
            sq = k.alloc("sq1", [128, 2, TW], BF16, stack=ph)
            glog = k.alloc("glog", [128, NCHUNK, 8], F32, stack=ph)
            gx = k.alloc("gx", [128, 2, 8], F32, stack=ph)
            gfm = k.alloc("gfm", [128, T], BF16, stack=ph)
            ones_rowb = k.alloc("ones_rowb", [1, 128], BF16, stack=ph)
            sel = k.alloc("sel", [8, 8, 128], BF16, stack=ph)
            ph2 = ExitStack()
            rwS = k.alloc("rwS", [16, D], BF16, stack=ph2)
            rwT = k.alloc("rwT", [128, NCHUNK, 16], BF16, stack=ph2)
            gsc = k.alloc("gsc", [128, 8], F32, stack=ph2)
            gcast = k.alloc("gcast", [128, 128], BF16, stack=ph2)

            k.vop("memset", ones_rowb[:], 1.0, writes=(B("ones_rowb"),))
            k.vop("memset", gcast[:, 8:128], 0.0, writes=(B("gcz"),))
            sel_np = np.zeros((8, 8, 128), dtype=ml_dtypes.bfloat16)
            for e8 in range(8):
                sel_np[e8, e8, :] = 1.0
            sel_const = nc.inline_tensor(sel_np.reshape(8, 8 * 128), name="sel_const")
            k.dma("g", sel[:].rearrange("p a b -> p (a b)"), sel_const.ap(), "dsel",
                  writes=(B("sel"),))
            layernorm(xn, "xn1", sq, "sq1", "ln1")

            k.vop("memset", rwS[:], 0.0, writes=(B("rwSz"),))
            k.dma("g", rwS[0:8, :], router_w, "drw", deps=(r.B("rwSz").w,),
                  writes=(B("rwS"),))
            k.dma_group("y",
                        [(rwT[:, :, :], rwS[:].rearrange("p (j m) -> p j m", m=128), True)],
                        "drwT", reads=(B("rwS"), B("rwSz")),
                        writebufs=(B("rwT"),))

            for tc in range(NCHUNK):
                cs = slice(tc * 128, (tc + 1) * 128)
                for j in range(NCHUNK):
                    k.mm(PS[:, 4 * TW:4 * TW + 8], xn[:, j, cs], rwT[:, j, 0:8],
                         j == 0, j == NCHUNK - 1, B("psum", 4),
                         reads=(B("xn1", j, tc // 4), B("rwT")))
                k.act(glog[:, tc, :], PS[:, 4 * TW:4 * TW + 8], AF.Copy,
                      reads=(B("psum", 4),), writes=(B("glog", tc),))
                L = glog[:, tc, :]
                EQ = gx[:, 0, :]
                L2 = gx[:, 1, :]
                k.vop("tensor_reduce", gsc[:, 0:1], L, AX.X, ALU.max,
                      reads=(B("glog", tc),), writes=(B("gm1"),))
                k.vop("tensor_single_scalar", EQ, L, gsc[:, 0:1], ALU.is_ge,
                      reads=(B("glog", tc), B("gm1")), writes=(B("geq"),))
                k.vop("scalar_tensor_tensor", L2, EQ, -1e9, L, ALU.mult, ALU.add,
                      reads=(B("geq"), B("glog", tc)), writes=(B("gl2"),))
                k.vop("tensor_reduce", gsc[:, 1:2], L2, AX.X, ALU.max,
                      reads=(B("gl2"),), writes=(B("gm2"),))
                k.vop("tensor_single_scalar", EQ, L, gsc[:, 1:2], ALU.is_ge,
                      reads=(B("glog", tc), B("gm2")), writes=(B("geq"),))
                k.vop("tensor_single_scalar", gsc[:, 2:3], gsc[:, 0:1], -1.0, ALU.mult,
                      reads=(B("gm1"),), writes=(B("gnm1"),))
                k.act(L2, L, AF.Exp, bias=gsc[:, 2:3],
                      reads=(B("glog", tc), B("gnm1")), writes=(B("gl2"),))
                k.vop("tensor_mul", L, L2, EQ,
                      reads=(B("gl2"), B("geq")), writes=(B("glog", tc),))
                k.vop("tensor_reduce", gsc[:, 3:4], L, AX.X, ALU.add,
                      reads=(B("glog", tc),), writes=(B("gz"),))
                k.vop("reciprocal", gsc[:, 3:4], gsc[:, 3:4],
                      reads=(B("gz"),), writes=(B("gz"),))
                k.vop("tensor_single_scalar", gcast[:, 0:8], L, gsc[:, 3:4], ALU.mult,
                      reads=(B("glog", tc), B("gz")), writes=(B("gc8"),))
                k.dma_group("y", [(gfm[:, cs], gcast[:], True)], "dgfm",
                            reads=(B("gc8"), B("gcz")),
                            writebufs=(B("gfm", tc),))

            ph2.close()
            yact = k.alloc("yact", [128, 11, T // 2], BF16, stack=ph)
            W1S = k.alloc("W1S", [128, 2, D], BF16, stack=ph)
            W1T = k.alloc("W1T", [128, 4, NCHUNK, 128], BF16, stack=ph)
            W2S = k.alloc("W2S", [128, 1, 1408], BF16, stack=ph)
            W2T = k.alloc("W2T", [128, 2, 11, 128], BF16, stack=ph)
            gb = k.alloc("gb", [128, T // 2], BF16, stack=ph)

            for ex in range(8):
                for hf in range(2):
                    hs = slice(hf * (T // 2), (hf + 1) * (T // 2))
                    for q in range(2):
                        tt = hf * 2 + q
                        ts = slice(tt * TW, (tt + 1) * TW)
                        k.mm(bank(6), sel[0:8, ex, :], gfm[0:8, ts], True, True,
                             B("psum", 6), reads=(B("sel"), B("gfm", tt * 4),
                                                  B("gfm", tt * 4 + 1), B("gfm", tt * 4 + 2),
                                                  B("gfm", tt * 4 + 3)))
                        k.act(gb[:, q * TW:(q + 1) * TW], bank(6), AF.Copy,
                              reads=(B("psum", 6),), writes=(B("gb", q),))
                    for c in range(11):
                        slA = (c % 2) * 2
                        slB = slA + 1
                        for (rowbase, sl) in ((c * 128, slA), (1408 + c * 128, slB)):
                            ss = 0 if sl < 2 else 1
                            k.dma("g", W1S[:, ss, :],
                                  bass.AP(w1.tensor, ex * 2816 * D + rowbase * D,
                                          [[D, 128], [1, D]]),
                                  f"dW1S{ss}", writes=(B("W1S", ss),))
                            k.dma_group("y",
                                        [(W1T[:, sl, :, :],
                                          W1S[:, ss, :].rearrange("p (j m) -> p j m", m=128), True)],
                                        f"dW1T{ss}", reads=(B("W1S", ss),),
                                        writebufs=(B("W1T", sl),))
                        for q in range(2):
                            tt = hf * 2 + q
                            ts = slice(tt * TW, (tt + 1) * TW)
                            lq = slice(q * TW, (q + 1) * TW)
                            bkG = q * 2
                            bkU = q * 2 + 1
                            for j in range(NCHUNK):
                                k.mm(bank(bkG), W1T[:, slA, j, :], xn[:, j, ts],
                                     j == 0, j == NCHUNK - 1, B("psum", bkG),
                                     reads=(B("W1T", slA), B("xn1", j, tt)))
                            for j in range(NCHUNK):
                                k.mm(bank(bkU), W1T[:, slB, j, :], xn[:, j, ts],
                                     j == 0, j == NCHUNK - 1, B("psum", bkU),
                                     reads=(B("W1T", slB), B("xn1", j, tt)))
                            k.act(sq[:, 0, :], bank(bkG), AF.Silu,
                                  reads=(B("psum", bkG),), writes=(B("sq1", 0),))
                            k.vop("tensor_mul", sq[:, 1, :], sq[:, 0, :], bank(bkU),
                                  reads=(B("sq1", 0), B("psum", bkU)), writes=(B("sq1", 1),))
                            k.vop("tensor_mul", yact[:, c, lq], sq[:, 1, :],
                                  gb[:, q * TW:(q + 1) * TW],
                                  reads=(B("sq1", 1), B("gb", q)), writes=(B("yact", c),))
                    for hc in range(NCHUNK):
                        sl = hc % 2
                        k.dma("g", W2S[:, 0, :],
                              bass.AP(w2.tensor, ex * D * 1408 + hc * 128 * 1408,
                                      [[1408, 128], [1, 1408]]),
                              "dW2S0", writes=(B("W2S", 0),))
                        k.dma_group("y",
                                    [(W2T[:, sl, :, :],
                                      W2S[:, 0, :].rearrange("p (i m) -> p i m", m=128), True)],
                                    f"dW2T{sl}", reads=(B("W2S", 0),),
                                    writebufs=(B("W2T", sl),))
                        for q in range(2):
                            tt = hf * 2 + q
                            ts = slice(tt * TW, (tt + 1) * TW)
                            lq = slice(q * TW, (q + 1) * TW)
                            bk = 4 + q
                            for i in range(11):
                                k.mm(bank(bk), W2T[:, sl, i, :], yact[:, i, lq],
                                     i == 0, i == 10, B("psum", bk),
                                     reads=(B("W2T", sl), B("yact", i)))
                            k.vop("tensor_add", R[:, hc, ts], bank(bk), R[:, hc, ts],
                                  reads=(B("psum", bk), B("R", hc, tt)),
                                  writes=(B("R", hc, tt),))
        r.barrier()

    if "l1_attn" in phases:
        with ExitStack() as ph:
            xn = k.alloc("xn2", [128, NCHUNK, T], BF16, stack=ph)
            sq = k.alloc("sq2", [128, 2, TW], BF16, stack=ph)
            Oq = k.alloc("Oq2", [128, 4, T], BF16, stack=ph)
            Otmp = k.alloc("Otmp", [64, TW], BF16, stack=ph)
            QQ = k.alloc("QQ", [128, 2, TW], BF16, stack=ph)
            KK = k.alloc("KK2", [128, T], BF16, stack=ph)
            VV = k.alloc("VV2", [128, NCHUNK, 130], BF16, stack=ph)
            PB = k.alloc("PB", [128, 4, 2, TW], BF16, stack=ph)
            WS2 = k.alloc("WS2", [128, 4, NCHUNK, 128], BF16, stack=ph)
            WO2 = k.alloc("WO2", [128, 2, 4, 128], BF16, stack=ph)
            nc2 = k.alloc("nc2", [128, 4], F32, stack=ph)

            layernorm(xn, "xn2", sq, "sq2", "ln2")
            SC2 = 0.125

            k.vop("memset", VV[:, :, 64:65], 1.0, writes=(B("vvone1"),))
            k.vop("memset", VV[:, :, 129:130], 1.0, writes=(B("vvone2"),))

            for gp in range(4):
                g1, g2 = gp, gp + 4
                for (wsrc, s4, nm) in ((Wk, 0, "wk"), (Wv, 1, "wv")):
                    items = []
                    for ig, gg in enumerate((g1, g2)):
                        sap = bass.AP(wsrc.tensor, gg * 64,
                                      [[512, 128], [128 * 512, NCHUNK], [1, 64]])
                        items.append((WS2[:, s4, :, ig * 64:(ig + 1) * 64], sap, False))
                    k.dma_group("g", items, f"dws2_{s4}", writebufs=(B("WS2", s4),))
                for tt in range(NT):
                    bk = tt % 2
                    ts = slice(tt * TW, (tt + 1) * TW)
                    for ig, tp in ((0, 0), (1, 64)):
                        for j in range(NCHUNK):
                            k.mm(bankrow(bk, tp, tp + 64), WS2[:, 0, j, ig * 64:(ig + 1) * 64],
                                 xn[:, j, ts], j == 0, j == NCHUNK - 1, B("psum", bk),
                                 reads=(B("WS2", 0), B("xn2", j, tt)),
                                 tile_position=(0, tp))
                    k.vop("tensor_copy", KK[:, ts], bank(bk),
                          reads=(B("psum", bk),), writes=(B("KK2", tt),))
                for kc in range(NCHUNK):
                    bk = 2 + kc % 2
                    for ig in range(2):
                        oslice = PS[:, bk * TW + ig * 64:bk * TW + (ig + 1) * 64]
                        for j in range(NCHUNK):
                            k.mm(oslice, xn[:, j, kc * 128:(kc + 1) * 128],
                                 WS2[:, 1, j, ig * 64:(ig + 1) * 64],
                                 j == 0, j == NCHUNK - 1, B("psum", bk),
                                 reads=(B("WS2", 1), B("xn2", j, kc // 4)))
                    k.vop("tensor_copy", VV[:, kc, 0:64],
                          PS[:, bk * TW:bk * TW + 64],
                          reads=(B("psum", bk),), writes=(B("VV2", kc, 0),))
                    k.vop("tensor_copy", VV[:, kc, 65:129],
                          PS[:, bk * TW + 64:bk * TW + 128],
                          reads=(B("psum", bk),), writes=(B("VV2", kc, 1),))
                for ig, tp in ((0, 0), (1, 64)):
                    mx = []
                    for tt in range(NT):
                        ts = slice(tt * TW, (tt + 1) * TW)
                        sl = tt % 2
                        k.vop("tensor_mul", sq[tp:tp + 64, sl, :], KK[tp:tp + 64, ts],
                              KK[tp:tp + 64, ts], reads=(B("KK2", tt),),
                              writes=(B("sq2", sl),))
                        k.mm(bankrow(4, 0, 1), ones_col[tp:tp + 64, :],
                             sq[tp:tp + 64, sl, :], True, True, B("psum", 4),
                             reads=(B("ones_col"), B("sq2", sl)),
                             tile_position=(tp, 0))
                        k.vop("tensor_reduce", smallf[0:1, tt:tt + 1], bankrow(4, 0, 1),
                              AX.X, ALU.max, reads=(B("psum", 4),), writes=(B("kn", tt),))
                        mx.append(B("kn", tt))
                    k.vop("tensor_reduce", smallf[0:1, 16 + ig:17 + ig], smallf[0:1, 0:4],
                          AX.X, ALU.max, reads=tuple(mx), writes=(B("knm", ig),))
                for s4 in range(4):
                    c0 = (g1 if s4 < 2 else g2) * 256 + (s4 % 2) * 128
                    sap = bass.AP(Wq.tensor, c0, [[D, 128], [128 * D, NCHUNK], [1, 128]])
                    k.dma("g", WS2[:, s4, :, :], sap, f"dws2_{s4}", writes=(B("WS2", s4),))
                steps = [(hh, tt) for hh in range(4) for tt in range(NT)]

                def qproj(i, ig):
                    hh, tt = steps[i]
                    par = i % 2
                    tp = 64 * ig
                    ts = slice(tt * TW, (tt + 1) * TW)
                    s4 = 2 * ig + hh // 2
                    mcol = (hh % 2) * 64
                    for j in range(NCHUNK):
                        k.mm(bankrow(7, tp, tp + 64), WS2[:, s4, j, mcol:mcol + 64],
                             xn[:, j, ts], j == 0, j == NCHUNK - 1, B("psum", 7),
                             reads=(B("WS2", s4), B("xn2", j, tt)),
                             tile_position=(0, tp))
                    k.vop("tensor_copy", QQ[tp:tp + 64, par, :],
                          bankrow(7, tp, tp + 64),
                          reads=(B("psum", 7),), writes=(B("QQ", par, ig),))
                    k.vop("tensor_mul", sq[tp:tp + 64, par, :], QQ[tp:tp + 64, par, :],
                          QQ[tp:tp + 64, par, :],
                          reads=(B("QQ", par, ig),), writes=(B("sq2", par, ig),))

                def c_chain(i, ig):
                    par = i % 2
                    tp = 64 * ig
                    k.mm(bankrow(4, 0, 1), ones_col[tp:tp + 64, :],
                         sq[tp:tp + 64, par, :], True, True, B("psum", 4),
                         reads=(B("ones_col"), B("sq2", par, ig)),
                         tile_position=(tp, 0))
                    k.vop("tensor_reduce", smallf[0:1, 20 + par:21 + par], bankrow(4, 0, 1),
                          AX.X, ALU.max, reads=(B("psum", 4),), writes=(B("qn", par),))
                    k.vop("tensor_add", smallf[0:1, 22 + par:23 + par],
                          smallf[0:1, 20 + par:21 + par],
                          smallf[0:1, 16 + ig:17 + ig],
                          reads=(B("qn", par), B("knm", ig)), writes=(B("cs", par),))
                    k.vop("tensor_scalar", smallf[0:1, 24 + par:25 + par],
                          smallf[0:1, 22 + par:23 + par],
                          -0.5 * SC2, -4.0, ALU.mult, ALU.add,
                          reads=(B("cs", par),), writes=(B("csn", par),))
                    k.mm(PS[:, 4 * TW + 64 + 2 * par + ig:4 * TW + 65 + 2 * par + ig],
                         ones_row[:], smallf[0:1, 24 + par:25 + par], True, True,
                         B("psum", 4), reads=(B("ones_row"), B("csn", par)))
                    k.act(nc2[:, 2 * par + ig:2 * par + ig + 1],
                          PS[:, 4 * TW + 64 + 2 * par + ig:4 * TW + 65 + 2 * par + ig],
                          AF.Copy, reads=(B("psum", 4),), writes=(B("nc2", par, ig),))

                qproj(0, 0)
                qproj(0, 1)
                c_chain(0, 0)
                c_chain(0, 1)
                for i, (hh, tt) in enumerate(steps):
                    par = i % 2
                    ts = slice(tt * TW, (tt + 1) * TW)

                    def o_mm(ig, j):
                        ob = 5 + ig
                        sl = 2 * ig + (j // 2) % 2
                        k.mm(PS[0:65, ob * TW:ob * TW + TW],
                             VV[:, j, ig * 65:(ig + 1) * 65], PB[:, sl, j % 2, :],
                             j == 0, j == NCHUNK - 1, B("psum", ob),
                             reads=(B("VV2", j, ig), B("vvone1"), B("vvone2"),
                                    B("PP", ig, j)))
                    if i + 1 < len(steps):
                        qproj(i + 1, 0)
                        qproj(i + 1, 1)
                    for j in range(NCHUNK):
                        for ig, tp in ((0, 0), (1, 64)):
                            bk = (j % 2) + 2 * ig
                            k.mm(bank(bk), KK[tp:tp + 64, j * 128:(j + 1) * 128],
                                 QQ[tp:tp + 64, par, :], True, True, B("psum", bk),
                                 reads=(B("KK2", j // 4), B("QQ", par, ig)),
                                 tile_position=(tp, 0))
                        if j % 2 == 1:
                            for ig in (0, 1):
                                b0 = 2 * ig
                                sl = 2 * ig + ((j - 1) // 2) % 2
                                k.act(PB[:, sl, :, :].rearrange("p a b -> p (a b)"),
                                      PS[:, b0 * TW:(b0 + 2) * TW], AF.Exp,
                                      bias=nc2[:, 2 * par + ig:2 * par + ig + 1], scale=SC2,
                                      reads=(B("psum", b0), B("psum", b0 + 1),
                                             B("nc2", par, ig)),
                                      writes=(B("PP", ig, j - 1), B("PP", ig, j)))
                            if j >= 3:
                                for ig in (0, 1):
                                    o_mm(ig, j - 3)
                                    o_mm(ig, j - 2)
                    for ig in (0, 1):
                        o_mm(ig, 14)
                        o_mm(ig, 15)
                    if i + 1 < len(steps):
                        c_chain(i + 1, 0)
                        c_chain(i + 1, 1)
                    for ig in (0, 1):
                        ob = 5 + ig
                        gg = g1 if ig == 0 else g2
                        habs = 4 * gg + hh
                        fcL = (0 if ig == 0 else 2) + hh // 2
                        k.vop("reciprocal", zr[:],
                              PS[64:65, ob * TW:ob * TW + TW],
                              reads=(B("psum", ob),), writes=(B("zr"),))
                        k.mm(bankrow(4, 0, 64), ones_row[:, 0:64], zr[:], True, True,
                             B("psum", 4), reads=(B("ones_row"), B("zr")))
                        k.act(zb[0:64, :], bankrow(4, 0, 64), AF.Copy,
                              reads=(B("psum", 4),), writes=(B("zb"),))
                        if habs % 2 == 0:
                            k.vop("tensor_mul", Oq[0:64, fcL, ts],
                                  PS[0:64, ob * TW:ob * TW + TW], zb[0:64, :],
                                  reads=(B("psum", ob), B("zb")),
                                  writes=(B("Oq2", fcL, 0, tt),))
                        else:
                            k.vop("tensor_mul", Otmp[:],
                                  PS[0:64, ob * TW:ob * TW + TW], zb[0:64, :],
                                  reads=(B("psum", ob), B("zb")),
                                  writes=(B("Otmp"),))
                            k.dma("y", Oq[64:128, fcL, ts], Otmp[:], "dOq",
                                  reads=(B("Otmp"),), writes=(B("Oq2", fcL, 1, tt),))
                for jd in range(NCHUNK):
                    wsl = jd % 2
                    items = []
                    for half in range(2):
                        rbase = (256 * gp if half == 0 else 1024 + 256 * gp)
                        sap = bass.AP(Wo.tensor, rbase * D + jd * 128,
                                      [[D, 128], [128 * D, 2], [1, 128]])
                        items.append((WO2[:, wsl, 2 * half:2 * half + 2, :], sap, False))
                    k.dma_group("g", items, f"dwo2_{wsl}", writebufs=(B("WO2", wsl),))
                    for tt in range(NT):
                        bk = tt % 2
                        ts = slice(tt * TW, (tt + 1) * TW)
                        for fc in range(4):
                            k.mm(bank(bk), WO2[:, wsl, fc, :], Oq[:, fc, ts],
                                 fc == 0, fc == 3, B("psum", bk),
                                 reads=(B("WO2", wsl), B("Oq2", fc, 0, tt),
                                        B("Oq2", fc, 1, tt)))
                        k.vop("tensor_add", R[:, jd, ts], bank(bk), R[:, jd, ts],
                              reads=(B("psum", bk), B("R", jd, tt)),
                              writes=(B("R", jd, tt),))
        r.barrier()

    if "l1_ffn" in phases:
        with ExitStack() as ph:
            xnq = k.alloc("xnq", [128, NCHUNK, TW], BF16, stack=ph)
            sq = k.alloc("sq3", [128, 2, TW], BF16, stack=ph)
            hq = k.alloc("hq", [128, 64, TW], BF16, stack=ph)
            WGT = k.alloc("WGT", [128, 3, NCHUNK, 128], BF16, stack=ph)
            QU = k.alloc("QU", [128, 16384], BF16, stack=ph)
            WPT = QU[:, 0:8192].rearrange("p (s i m) -> p s i m", s=2, m=128)
            QS = QU[:, 0:8192].rearrange("p (s x) -> p s x", s=4)
            QT = QU[:, 8192:16384].rearrange("p (s j m) -> p s j m", s=4, m=128)
            qa = k.alloc("qa", [128, 128], F32, stack=ph)
            qsb = k.alloc("qsb", [128, 128], BF16, stack=ph)
            qsf = k.alloc("qsf", [128, 128], F32, stack=ph)
            qr = k.alloc("qr", [128, 128], F32, stack=ph)
            hsq = k.alloc("hsq", [128, 2, TW], BF16, stack=ph)
            zbq = zb

            qslot = [0]

            def quant_full(src_ap, outbuf, store_dst):
                ng = 32
                slot = qslot[0] % 4
                qslot[0] += 1
                S = QS[:, slot, :]
                S3 = QS[:, slot, :].rearrange("p (g x) -> p g x", x=64)
                A = qa[:, slot * ng:(slot + 1) * ng]
                SB = qsb[:, slot * ng:(slot + 1) * ng]
                SF = qsf[:, slot * ng:(slot + 1) * ng]
                RC = qr[:, slot * ng:(slot + 1) * ng]
                k.dma("g", S, src_ap, f"dQS{slot}", writes=(B("QS", slot),))
                k.vop("tensor_reduce", A, S3, AX.X, ALU.add, apply_absolute_value=True,
                      reads=(B("QS", slot),), writes=(B("qa", slot),))
                k.vop("tensor_scalar", SB, A, 1.0 / 64.0, 1e-8, ALU.mult, ALU.max,
                      reads=(B("qa", slot),), writes=(B("qsb", slot),))
                k.vop("tensor_copy", SF, SB, reads=(B("qsb", slot),), writes=(B("qsf", slot),))
                k.vop("reciprocal", RC, SF, reads=(B("qsf", slot),), writes=(B("qr", slot),))
                rb = RC.unsqueeze(2).broadcast_to([128, ng, 64])
                sb = SF.unsqueeze(2).broadcast_to([128, ng, 64])
                k.vop("tensor_tensor", S3, S3, rb, ALU.mult,
                      reads=(B("QS", slot), B("qr", slot)), writes=(B("QS", slot),))
                k.vop("tensor_scalar", S, S, MAGIC, -MAGIC, ALU.add, ALU.add,
                      reads=(B("QS", slot),), writes=(B("QS", slot),))
                k.vop("tensor_scalar", S, S, 1.0, -1.0, ALU.min, ALU.max,
                      reads=(B("QS", slot),), writes=(B("QS", slot),))
                k.vop("tensor_tensor", S3, S3, sb, ALU.mult,
                      reads=(B("QS", slot), B("qsf", slot)), writes=(B("QS", slot),))
                k.dma_group("y",
                            [(QT[:, slot, :, :],
                              QS[:, slot, :].rearrange("p (j m) -> p j m", m=128), True)],
                            f"dQT{slot}", reads=(B("QS", slot),),
                            writebufs=(B("QT", slot),))
                k.dma("s", store_dst, QT[:, slot, :, :], f"dQW{slot}",
                      reads=(B("QT", slot),), writes=(outbuf,))
                return slot

            def q_wgu(rb):
                sap = bass.AP(wgu.tensor, rb * 128 * D, [[D, 128], [1, D]])
                dst = bass.AP(sc_wgu.tensor, rb * D * 128,
                              [[128, 128], [128 * 128, NCHUNK], [1, 128]])
                return quant_full(sap, B("scwgu", rb), dst)

            def q_wp(dc, hx):
                sap = bass.AP(wp.tensor, dc * 128 * 8192 + hx * D, [[8192, 128], [1, D]])
                dst = bass.AP(sc_wp.tensor, dc * 8192 * 128 + hx * D * 128,
                              [[128, 128], [128 * 128, NCHUNK], [1, 128]])
                return quant_full(sap, B("scwp", dc, hx), dst)

            for qt in range(NT):
                ts = slice(qt * TW, (qt + 1) * TW)
                layernorm(xnq, "xnq", sq, "sq3", f"ln3_{qt}", tts=[qt], toff=qt)
                for c in range(64):
                    if qt == 0:
                        sG = q_wgu(c)
                        sU = q_wgu(64 + c)

                        def gtile(j, s=sG):
                            return QT[:, s, j, :]

                        def utile(j, s=sU):
                            return QT[:, s, j, :]
                        gbufs = (B("QT", sG),)
                        ubufs = (B("QT", sU),)
                    else:
                        slG = (2 * c) % 3
                        slU = (2 * c + 1) % 3
                        k.dma("y", WGT[:, slG, :, :],
                              bass.AP(sc_wgu.tensor, c * D * 128, [[128, 128], [128 * 128, NCHUNK], [1, 128]]),
                              f"dWGT{slG}", reads=(B("scwgu", c),), writes=(B("WGT", slG),))
                        k.dma("y", WGT[:, slU, :, :],
                              bass.AP(sc_wgu.tensor, (64 + c) * D * 128, [[128, 128], [128 * 128, NCHUNK], [1, 128]]),
                              f"dWGT{slU}", reads=(B("scwgu", 64 + c),), writes=(B("WGT", slU),))
                        def gtile(j, s=slG):
                            return WGT[:, s, j, :]

                        def utile(j, s=slU):
                            return WGT[:, s, j, :]
                        gbufs = (B("WGT", slG),)
                        ubufs = (B("WGT", slU),)
                    bkG = (c % 2) * 2
                    bkU = bkG + 1
                    for j in range(NCHUNK):
                        k.mm(bank(bkG), gtile(j), xnq[:, j, :],
                             j == 0, j == NCHUNK - 1, B("psum", bkG),
                             reads=gbufs + (B("xnq", j, qt),))
                    for j in range(NCHUNK):
                        k.mm(bank(bkU), utile(j), xnq[:, j, :],
                             j == 0, j == NCHUNK - 1, B("psum", bkU),
                             reads=ubufs + (B("xnq", j, qt),))
                    k.act(sq[:, 0, :], bank(bkG), AF.Silu,
                          reads=(B("psum", bkG),), writes=(B("sq3", 0),))
                    k.vop("tensor_mul", hq[:, c, :], sq[:, 0, :], bank(bkU),
                          reads=(B("sq3", 0), B("psum", bkU)), writes=(B("hq", c),))
                    k.vop("tensor_mul", sq[:, 1, :].rearrange("p x -> p x") if False else hsq[:, c % 2, :], hq[:, c, :], hq[:, c, :],
                          reads=(B("hq", c),), writes=(B("hsq", c % 2),))
                    if c > 0:
                        k.mm(bankrow(4, 0, 1), ones_col[:], hsq[:, (c - 1) % 2, :],
                             c == 1, False, B("psum", 4),
                             reads=(B("ones_col"), B("hsq", (c - 1) % 2)))
                k.mm(bankrow(4, 0, 1), ones_col[:], hsq[:, 63 % 2, :], False, True,
                     B("psum", 4), reads=(B("ones_col"), B("hsq", 63 % 2)))
                rsqrt_rows(f"rq{qt}", bankrow(4, 0, 1), B("psum", 4), 1.0 / 8192.0, RMS_EPS)
                k.mm(bank(5), ones_row[:], rows[:, 4, :], True, True, B("psum", 5),
                     reads=(B("ones_row"), B(f"rq{qt}", "m1")))
                k.act(zbq[:], bank(5), AF.Copy, reads=(B("psum", 5),), writes=(B("zbq"),))
                for dc in range(NCHUNK):
                    bk = 6 + dc % 2
                    if qt == 0:
                        for hx in range(4):
                            sW = q_wp(dc, hx)
                            for i in range(NCHUNK):
                                hi = hx * NCHUNK + i
                                k.mm(bank(bk), QT[:, sW, i, :], hq[:, hi, :],
                                     hi == 0, hi == 63, B("psum", bk),
                                     reads=(B("QT", sW), B("hq", hi)))
                    else:
                        for half in range(2):
                            wrb = (B("WPT", half),)
                            if qt == 1 and dc == 0:
                                wrb = wrb + tuple(B("QS", s) for s in range(4)) \
                                    + tuple(B("QT", s) for s in range(4))
                            k.dma("y", WPT[:, half, :, :],
                                  bass.AP(sc_wp.tensor, dc * 8192 * 128 + half * 4096 * 128,
                                          [[128, 128], [128 * 128, 32], [1, 128]]),
                                  f"dWPT{half}",
                                  reads=(B("scwp", dc, 2 * half), B("scwp", dc, 2 * half + 1)),
                                  writes=wrb)
                        for half in range(2):
                            for i in range(32):
                                hi = half * 32 + i
                                k.mm(bank(bk), WPT[:, half, i, :], hq[:, hi, :],
                                     hi == 0, hi == 63, B("psum", bk),
                                     reads=(B("WPT", half), B("hq", hi)))
                    k.vop("tensor_mul", sq[:, 1, :], bank(bk), zbq[:],
                          reads=(B("psum", bk), B("zbq")), writes=(B("sq3", 1),))
                    k.vop("tensor_add", R[:, dc, ts], sq[:, 1, :], R[:, dc, ts],
                          reads=(B("sq3", 1), B("R", dc, qt)),
                          writes=(B("R", dc, qt),))
        r.barrier()

    with ExitStack() as ph:
        stg = k.alloc("stg_out", [128, 2, D], BF16, stack=ph)
        for tt in range(16):
            sl = tt % 2
            stgap = stg[:, sl, :]
            k.dma_group("y",
                        [(stgap[:, j * 128:(j + 1) * 128], R[:, j, tt * 128:(tt + 1) * 128], True)
                         for j in range(NCHUNK)],
                        f"dOstg{sl}",
                        reads=tuple(B("R", j, tt // 4) for j in range(NCHUNK)),
                        writebufs=(B("stg_out", sl),))
            k.dma("g", out[tt * 128:(tt + 1) * 128, :], stgap,
                  f"dOut{sl}", reads=(B("stg_out", sl),))

        def final(eng):
            eng.wait_ge(r.sems["dOut0"], r.cnt["dOut0"])
            eng.wait_ge(r.sems["dOut1"], r.cnt["dOut1"])
            return None
        r.sem("dOut0")
        r.sem("dOut1")
        r.ops["y"].append(((), final, "Ey", 0))
        r.sem("Ey")

    with es, nc.Block() as block:
        @block.tensor
        def _(eng):
            r.replay("t", eng)

        @block.vector
        def _(eng):
            r.replay("v", eng)

        @block.scalar
        def _(eng):
            r.replay("s", eng)

        @block.gpsimd
        def _(eng):
            r.replay("g", eng)

        @block.sync
        def _(eng):
            r.replay("y", eng)

    return nc


def kernel(L0_attn_x, L0_attn_w_qkv, L0_attn_b_qkv, L0_attn_w_out, L0_attn_b_out,
           L0_ffn_router_w, L0_ffn_w1, L0_ffn_w2,
           L1_attn_Wq, L1_attn_Wk, L1_attn_Wv, L1_attn_Wo,
           L1_ffn_w_gate_up, L1_ffn_w_proj):
    nc = build()
    common = {
        "w_qkv": np.ascontiguousarray(L0_attn_w_qkv, dtype=np.float32),
        "b_qkv": np.ascontiguousarray(L0_attn_b_qkv, dtype=np.float32),
        "w_out": np.ascontiguousarray(L0_attn_w_out, dtype=np.float32),
        "b_out": np.ascontiguousarray(L0_attn_b_out, dtype=np.float32),
        "router_w": np.ascontiguousarray(L0_ffn_router_w, dtype=np.float32),
        "w1": np.ascontiguousarray(L0_ffn_w1, dtype=np.float32),
        "w2": np.ascontiguousarray(L0_ffn_w2, dtype=np.float32),
        "Wq": np.ascontiguousarray(L1_attn_Wq, dtype=np.float32),
        "Wk": np.ascontiguousarray(L1_attn_Wk, dtype=np.float32),
        "Wv": np.ascontiguousarray(L1_attn_Wv, dtype=np.float32),
        "Wo": np.ascontiguousarray(L1_attn_Wo, dtype=np.float32),
        "wgu": np.ascontiguousarray(L1_ffn_w_gate_up, dtype=np.float32),
        "wp": np.ascontiguousarray(L1_ffn_w_proj, dtype=np.float32),
    }
    xs = np.ascontiguousarray(np.asarray(L0_attn_x, dtype=np.float32))
    in_maps = [dict(common, x=np.ascontiguousarray(xs[c])) for c in range(8)]
    res = run_bass_kernel_spmd(nc, in_maps, core_ids=list(range(8)))
    return np.stack([np.asarray(rr["out"], dtype=np.float32) for rr in res.results], axis=0)
